# version 01
# baseline (speedup 1.0000x reference)
import numpy as np
import ml_dtypes
import concourse.bass as bass
import concourse.mybir as mybir
from concourse.bass_utils import run_bass_kernel_spmd

F32 = mybir.dt.float32
BF16 = mybir.dt.bfloat16
I32 = mybir.dt.int32
AF = mybir.ActivationFunctionType
ALU = mybir.AluOpType
AX = mybir.AxisListType

B, S, D = 4, 2048, 4096
NH = 16
T = 1024
DCH = 32
KEXP, RANK = 64, 16
KR = KEXP * RANK
SCALE_QK = float(192.0 ** -0.5)
EPS = 1e-5
NEG_BIG = -1.0e30
INV_D = 1.0 / D

BUILD_UPTO = 99
ENGINES = ("sync", "tensor", "vector", "scalar", "gpsimd")


class Arena:
    def __init__(self, total):
        self.total = total
        self.bufs = {}

    def buf(self, name, off, cols):
        assert off + cols <= self.total, (name, off, cols, self.total)
        self.bufs[name] = (off, cols)

    def overlaps(self, name):
        o, c = self.bufs[name]
        return [n2 for n2, (o2, c2) in self.bufs.items()
                if n2 != name and o < o2 + c2 and o2 < o + c]


class Pr:

    def __init__(self):
        self.ops = {e: [] for e in ENGINES}
        self.cnt = {}
        self.semnames = []
        self.last_w = {}
        self.readers = {}
        self.hw = {e: {} for e in ENGINES}
        self.pe_pend = []

    def sem(self, name):
        if name not in self.cnt:
            self.cnt[name] = 0
            self.semnames.append(name)
        return name

    def _need(self, eng, tag, need):
        if tag is None:
            return
        if isinstance(tag, list):
            for t in tag:
                self._need(eng, t, need)
            return
        if tag[0] == 'PEPEND':
            assert eng == 'tensor', "consumer needs un-marked PE result"
            return
        s, v = tag
        if eng == 'tensor' and s == 'pe':
            return
        if self.hw[eng].get(s, 0) >= v:
            return
        need[s] = max(need.get(s, 0), v)

    @staticmethod
    def _k(key):
        return "ROWS" if key.startswith("r_") else key

    def op(self, eng, fn, reads=(), writes=(), sem=None, amt=0):
        reads = tuple(dict.fromkeys(self._k(k) for k in reads))
        writes = tuple(dict.fromkeys(self._k(k) for k in writes))
        need = {}
        for r in reads:
            self._need(eng, self.last_w.get(r), need)
        for w in writes:
            self._need(eng, self.last_w.get(w), need)
            for t in self.readers.get(w, ()):
                self._need(eng, t, need)
        for s, v in need.items():
            self.hw[eng][s] = max(self.hw[eng].get(s, 0), v)

        if sem is None and eng in ("vector", "scalar", "gpsimd") and fn is not None:
            sem = {"vector": "dve", "scalar": "act", "gpsimd": "pool"}[eng]
            amt = 1
        tag = None
        if sem is not None:
            self.sem(sem)
            self.cnt[sem] += amt
            tag = (sem, self.cnt[sem])

        self.ops[eng].append(dict(waits=dict(need), fn=fn, sem=sem, amt=amt))

        if eng == "tensor" and tag is None and fn is not None:
            for r in reads:
                lst = self.readers.setdefault(r, [])
                self.pe_pend.append(('r', r, len(lst)))
                lst.append(('PEPEND', 0))
            for w in writes:
                self.pe_pend.append(('w', w, -1))
                self.last_w[w] = ('PEPEND', 0)
                self.readers[w] = []
        elif tag is not None:
            for r in reads:
                self.readers.setdefault(r, []).append(tag)
            for w in writes:
                self.last_w[w] = tag
                self.readers[w] = []
        return tag

    def wait(self, eng, reads):
        self.op(eng, None, reads=tuple(reads))

    def mark_pe(self):
        assert self.ops["tensor"], "mark_pe with no PE ops"
        rec = self.ops["tensor"][-1]
        assert rec["sem"] is None, "double mark"
        self.sem("pe")
        self.cnt["pe"] += 1
        rec["sem"], rec["amt"] = "pe", 1
        tag = ("pe", self.cnt["pe"])
        for kind, res, i in self.pe_pend:
            if kind == 'w':
                lw = self.last_w.get(res)
                if isinstance(lw, tuple) and lw[0] == 'PEPEND':
                    self.last_w[res] = tag
            else:
                lst = self.readers.get(res, [])
                if i < len(lst) and isinstance(lst[i], tuple) and lst[i][0] == 'PEPEND':
                    lst[i] = tag
        self.pe_pend = []


def build_program():
    nc = bass.Bass("TRN2", target_bir_lowering=False, num_devices=8,
                   detect_race_conditions=False)

    def din(name, shape, dtype=F32):
        return nc.dram_tensor(name, list(shape), dtype, kind="ExternalInput").ap()

    XT = din("xt", (D, T))
    WRT = din("wrt", (D, KEXP))
    AAL = din("aal", (D, KR))
    BAL = din("bal", (KR, D))
    WQKV = din("wqkv", (3, D, NH * 128))
    WO = din("wo", (D, D))
    IDN32 = din("idn32", (128, 128))
    ONESROW = din("onesrow", (1, 128))
    ONESRB = din("onesrb", (1, 128), BF16)
    ONESB = din("onesb", (128, 2), BF16)
    MASKB = din("maskb", (128, 896), BF16)
    REP = din("rep", (64, KR), BF16)
    NONCE = din("nonce", (1, 8), I32)

    OUT = nc.dram_tensor("out", [D, T], F32, kind="ExternalOutput").ap()

    WQKVBF = nc.dram_tensor("wqkvbf", [3, NH, 128, DCH * 128], BF16,
                            kind="Internal").ap()
    XSP = nc.dram_tensor("xsp", [DCH, 128, T], BF16, kind="Internal").ap()
    Y2L = nc.dram_tensor("y2l", [DCH, 128, T], BF16, kind="Internal").ap()
    OL = nc.dram_tensor("ol", [NH, 128, S], BF16, kind="Internal").ap()
    SHY = nc.dram_tensor("shy", [2, DCH, 128, T], BF16, kind="Internal",
                         addr_space="Shared").ap()
    OSH = nc.dram_tensor("osh", [2, NH, 128, S], BF16, kind="Internal",
                         addr_space="Shared").ap()
    SHF = nc.dram_tensor("shf", [4, 2, 1, 2], I32, kind="Internal",
                         addr_space="Shared").ap()

    A16 = Arena(92736)
    A16.buf("maskb", 0, 896)
    A16.buf("onesb", 896, 2)
    A16.buf("onesrb", 928, 128)
    A16.buf("xtb", 1056, DCH * T)
    A16.buf("yh", 33824, DCH * 512)
    A16.buf("wrtb", 50208, 2048)
    A16.buf("repb", 52256, 1024)
    A16.buf("gexp", 53280, 4096)
    A16.buf("y2stg", 53280, 2048)
    A16.buf("hg", 57376, 4096)
    A16.buf("xsq", 89632, 1024)
    A16.buf("row16", 62496, 1024)
    A16.buf("gts", 63520, 512)
    A16.buf("pnl16", 81440, 8192)
    A16.buf("ytmp", 72224, 512)
    A16.buf("rowb", 72736, 2048)
    A16.buf("y2f", 1056, DCH * S)
    A16.buf("qbuf", 66592, 4096)
    A16.buf("kbuf", 70688, 4096)
    A16.buf("vtok", 74784, 4096)
    A16.buf("vtmp", 78880, 512)
    A16.buf("usq", 78880, 512)
    A16.buf("ebuf", 79392, 2048)
    A16.buf("qkvp", 81440, 8192)
    A16.buf("ostg", 89632, 1024)
    A16.buf("pc16", 90656, 2048)
    A16.buf("ores", 33824, DCH * T)

    A32 = Arena(6528)
    A32.buf("idn", 0, 128)
    A32.buf("onesrow", 128, 128)
    A32.buf("xstage", 256, 2048)
    A32.buf("ostage", 256, 1024)
    A32.buf("pcst", 2304, 2048)
    A32.buf("pnl32", 2304, 2048)
    A32.buf("rows", 4352, 1024)
    A32.buf("gat", 5376, 512)
    A32.buf("rz", 5888, 640)

    prog = Pr()
    for s in ("pe", "dve", "act", "pool"):
        prog.sem(s)

    def Rk(region, idx=None):
        if idx is None:
            return (region,)
        return ("%s.%s" % (region, idx), region)

    def Wk(arena, region, idx=None):
        return tuple(list(Rk(region, idx)) + arena.overlaps(region))

    W16 = lambda region, idx=None: Wk(A16, region, idx)
    W32 = lambda region, idx=None: Wk(A32, region, idx)

    import contextlib
    est = contextlib.ExitStack()
    AR16_t = est.enter_context(nc.sbuf_tensor("AR16", [128, A16.total], BF16))
    AR32_t = est.enter_context(nc.sbuf_tensor("AR32", [128, A32.total], F32))
    SNON_t = est.enter_context(nc.sbuf_tensor("SNON", [1, 8], I32))
    SFLG_t = est.enter_context(nc.sbuf_tensor("SFLG", [1, 4], I32))
    pM_t = est.enter_context(nc.psum_tensor("pM", [128, 1024], F32))
    pS_t = est.enter_context(nc.psum_tensor("pS", [128, 2048], F32))
    pZ_t = est.enter_context(nc.psum_tensor("pZ", [1, 512], F32))
    pU_t = est.enter_context(nc.psum_tensor("pU", [1, 512], F32))
    flpoll = est.enter_context(nc.semaphore("flpoll"))

    AR16, AR32 = AR16_t[:], AR32_t[:]
    SNON, SFLG = SNON_t[:], SFLG_t[:]
    pM, pS, pZ, pU = pM_t[:], pS_t[:], pZ_t[:], pU_t[:]

    def a16(name, p0=0, p1=128, off=0, cols=None):
        o, c = A16.bufs[name]
        if cols is None:
            cols = c - off
        return AR16[p0:p1, o + off:o + off + cols]

    def a32(name, p0=0, p1=128, off=0, cols=None):
        o, c = A32.bufs[name]
        if cols is None:
            cols = c - off
        return AR32[p0:p1, o + off:o + off + cols]

    dmaq = {}

    def dma(q, semname, fn, reads=(), writes=()):
        assert dmaq.setdefault(semname, q) == q, semname
        prog.op(q, fn, reads=reads, writes=writes, sem=prog.sem(semname), amt=16)

    def dma_pair(q, semname, fn_even, fn_odd, reads=(), writes=()):
        assert dmaq.setdefault(semname, q) == q, semname
        def f(eng, rt):
            return [fn_even(eng, rt), fn_odd(eng, rt)]
        prog.op(q, f, reads=reads, writes=writes, sem=prog.sem(semname), amt=32)

    def pe(fn, reads=(), writes=()):
        prog.op("tensor", fn, reads=reads, writes=writes)

    def dve(fn, reads=(), writes=()):
        prog.op("vector", fn, reads=reads, writes=writes)

    def act(fn, reads=(), writes=()):
        prog.op("scalar", fn, reads=reads, writes=writes)

    def pool(fn, reads=(), writes=()):
        prog.op("gpsimd", fn, reads=reads, writes=writes)

    mark = prog.mark_pe
    RZ_NEGU, RZ_A, RZ_B, RZ_C = 512, 516, 520, 524

    dma("sync", "d_non", lambda e, rt: e.dma_start(out=SNON, in_=NONCE),
        writes=("snon",))
    for nm, dst, src in (
        ("maskb", a16("maskb"), MASKB),
        ("onesb", a16("onesb"), ONESB),
        ("onesrb", a16("onesrb", p1=1), ONESRB),
        ("idn", a32("idn"), IDN32),
        ("onesrow", a32("onesrow", p1=1, cols=128), ONESROW),
        ("repb", a16("repb", p1=64), REP),
    ):
        dma("sync", "d_cst",
            lambda e, rt, d=dst, s=src: e.dma_start(out=d, in_=s),
            writes=(nm,))

    dma("sync", "d_x0",
        lambda e, rt: e.dma_start(
            out=a32("xstage", cols=2048).rearrange("p (c k) -> p c k", c=DCH),
            in_=WRT.rearrange("(c p) k -> p c k", p=128)),
        writes=W32("xstage", 0) + W32("xstage", 1))
    pool(lambda e, rt: e.tensor_copy(out=a16("wrtb"),
                                     in_=a32("xstage", cols=2048)),
         reads=Rk("xstage", 0) + Rk("xstage", 1), writes=W16("wrtb"))

    for ch in range(DCH):
        sl = ch % 2
        dma("sync", "d_x%d" % sl,
            lambda e, rt, ch=ch, sl=sl: e.dma_start(
                out=a32("xstage", off=1024 * sl, cols=1024),
                in_=XT[128 * ch:128 * (ch + 1), :]),
            writes=W32("xstage", sl))
        pool(lambda e, rt, ch=ch, sl=sl: e.tensor_copy(
                out=a16("xtb", off=T * ch, cols=T),
                in_=a32("xstage", off=1024 * sl, cols=1024)),
             reads=Rk("xstage", sl), writes=W16("xtb", ch))
        for hh in range(2):
            pool(lambda e, rt, ch=ch, hh=hh: e.tensor_mul(
                    out=a16("xsq", off=512 * hh, cols=512),
                    in0=a16("xtb", off=T * ch + 512 * hh, cols=512),
                    in1=a16("xtb", off=T * ch + 512 * hh, cols=512)),
                 reads=Rk("xtb", ch), writes=W16("xsq", hh))
            ps_sum = pZ if hh == 0 else pS[0:1, 0:512]
            ps_sq = pU if hh == 0 else pS[0:1, 512:1024]
            k_sum = "pZ" if hh == 0 else "pS.0"
            k_sq = "pU" if hh == 0 else "pS.1"
            pe(lambda e, rt, ch=ch, hh=hh, ps=ps_sum: e.matmul(
                    ps, a16("onesb", cols=1),
                    a16("xtb", off=T * ch + 512 * hh, cols=512),
                    start=(ch == 0), stop=(ch == DCH - 1)),
               reads=Rk("xtb", ch) + ("onesb",), writes=(k_sum,))
            pe(lambda e, rt, ch=ch, hh=hh, ps=ps_sq: e.matmul(
                    ps, a16("onesb", cols=1),
                    a16("xsq", off=512 * hh, cols=512),
                    start=(ch == 0), stop=(ch == DCH - 1)),
               reads=Rk("xsq", hh) + ("onesb",), writes=(k_sq,))
            mark()

    def emit_precast(h, p3):
        for hf in range(2):
            dma("sync", "d_pc",
                lambda e, rt, h=h, p3=p3, hf=hf: e.dma_start(
                    out=a32("pcst", cols=2048).rearrange("p (c k) -> p c k", c=16),
                    in_=WQKV[p3].rearrange("(c p) n -> p c n", p=128)[
                        :, 16 * hf:16 * (hf + 1), 128 * h:128 * (h + 1)]),
                writes=W32("pcst"))
            pool(lambda e, rt: e.tensor_copy(out=a16("pc16", cols=2048),
                                             in_=a32("pcst", cols=2048)),
                 reads=Rk("pcst"), writes=W16("pc16"))
            dma("gpsimd", "d_pcw",
                lambda e, rt, h=h, p3=p3, hf=hf: e.dma_start(
                    out=WQKVBF[p3, h][:, 2048 * hf:2048 * (hf + 1)],
                    in_=a16("pc16", cols=2048)),
                reads=Rk("pc16"), writes=("wbf.%d.%d" % (p3, h), "wbf"))

    pc_iter = [(h, p3) for h in range(NH) for p3 in range(3)]
    pc_pos = [0]

    def precast_some(k):
        for _ in range(k):
            if pc_pos[0] < len(pc_iter):
                emit_precast(*pc_iter[pc_pos[0]])
                pc_pos[0] += 1

    def ln_finalize(ps_sum, k_sum, ps_sq, k_sq, slot):
        rbo = 1024 * slot
        act(lambda e, rt: e.activation(out=a32("rows", p1=1, off=0, cols=512),
                                       in_=ps_sum, func=AF.Identity,
                                       scale=INV_D),
            reads=(), writes=(k_sum, "r_rm") + W32("rows"))
        act(lambda e, rt: e.activation(out=a32("rows", p1=1, off=512, cols=512),
                                       in_=ps_sq, func=AF.Identity,
                                       scale=INV_D),
            reads=(), writes=(k_sq, "r_rstd"))
        dve(lambda e, rt: e.tensor_scalar(
                out=a32("rows", p1=1, off=512, cols=512),
                in0=a32("rows", p1=1, off=512, cols=512),
                scalar1=EPS, scalar2=None, op0=ALU.add),
            reads=("r_rstd",), writes=("r_rstd",))
        dve(lambda e, rt: e.tensor_mul(out=a32("gat", p1=1, cols=512),
                                       in0=a32("rows", p1=1, off=0, cols=512),
                                       in1=a32("rows", p1=1, off=0, cols=512)),
            reads=("r_rm",), writes=W32("gat") + ("r_msq",))
        dve(lambda e, rt: e.tensor_sub(out=a32("rows", p1=1, off=512, cols=512),
                                       in0=a32("rows", p1=1, off=512, cols=512),
                                       in1=a32("gat", p1=1, cols=512)),
            reads=("r_msq", "r_rstd"), writes=("r_rstd",))
        act(lambda e, rt: e.activation(out=a32("gat", p1=1, cols=512),
                                       in_=a32("rows", p1=1, off=512, cols=512),
                                       func=AF.Ln),
            reads=("r_rstd",), writes=("r_msq",))
        act(lambda e, rt: e.activation(out=a32("rows", p1=1, off=512, cols=512),
                                       in_=a32("gat", p1=1, cols=512),
                                       func=AF.Exp, scale=-0.5),
            reads=("r_msq",), writes=("r_rstd",))
        dve(lambda e, rt: e.tensor_mul(out=a32("rows", p1=1, off=0, cols=512),
                                       in0=a32("rows", p1=1, off=0, cols=512),
                                       in1=a32("rows", p1=1, off=512, cols=512)),
            reads=("r_rm", "r_rstd"), writes=("r_rm",))
        dve(lambda e, rt: e.tensor_copy(out=a16("row16", p1=1, off=0, cols=512),
                                        in_=a32("rows", p1=1, off=512, cols=512)),
            reads=("r_rstd",), writes=W16("row16", 0))
        dve(lambda e, rt: e.tensor_copy(out=a16("row16", p1=1, off=512, cols=512),
                                        in_=a32("rows", p1=1, off=0, cols=512)),
            reads=("r_rm",), writes=W16("row16", 1))
        pe(lambda e, rt: e.matmul(pM[:, 0:512], a16("onesrb", p1=1),
                                  a16("row16", p1=1, off=0, cols=512),
                                  start=True, stop=True),
           reads=Rk("row16", 0) + ("onesrb",), writes=("pM.0",))
        mark()
        pe(lambda e, rt: e.matmul(pM[:, 512:1024], a16("onesrb", p1=1),
                                  a16("row16", p1=1, off=512, cols=512),
                                  start=True, stop=True),
           reads=Rk("row16", 1) + ("onesrb",), writes=("pM.1",))
        mark()
        dve(lambda e, rt, rbo=rbo: e.tensor_copy(
                out=a16("rowb", off=rbo, cols=512), in_=pM[:, 0:512]),
            reads=(), writes=("pM.0",) + W16("rowb", 2 * slot))
        dve(lambda e, rt, rbo=rbo: e.tensor_copy(
                out=a16("rowb", off=rbo + 512, cols=512), in_=pM[:, 512:1024]),
            reads=(), writes=("pM.1",) + W16("rowb", 2 * slot + 1))

    def ln_apply(ch, hh, slot, dst_ap_fn, dst_w):
        rbo = 1024 * slot
        dve(lambda e, rt, ch=ch, hh=hh, rbo=rbo: e.tensor_mul(
                out=a16("ytmp"),
                in0=a16("xtb", off=T * ch + 512 * hh, cols=512),
                in1=a16("rowb", off=rbo, cols=512)),
            reads=Rk("xtb", ch) + Rk("rowb", 2 * slot), writes=W16("ytmp"))
        dve(lambda e, rt, rbo=rbo: e.tensor_sub(
                out=dst_ap_fn(), in0=a16("ytmp"),
                in1=a16("rowb", off=rbo + 512, cols=512)),
            reads=("ytmp",) + Rk("rowb", 2 * slot + 1), writes=dst_w)

    def emit_ffn(hh, final, k_sum=None, k_sq=None, ps_sum=None, ps_sq=None,
                 precast_budget=0):
        for ch in range(DCH):
            pe(lambda e, rt, ch=ch: e.matmul(
                    pS[0:64, 0:512],
                    a16("wrtb", off=64 * ch, cols=64),
                    a16("yh", off=512 * ch, cols=512),
                    start=(ch == 0), stop=(ch == DCH - 1)),
               reads=Rk("yh", ch) + ("wrtb",), writes=("pS.0",))
        mark()
        dve(lambda e, rt: e.tensor_copy(out=a32("gat", p1=64),
                                        in_=pS[0:64, 0:512]),
            reads=(), writes=("pS.0",) + W32("gat") + ("r_msq",))
        for t4 in range(4):
            pe(lambda e, rt, t4=t4: e.transpose(
                    pS[:, 512:576],
                    a32("gat", p0=0, p1=64, off=128 * t4, cols=128),
                    a32("idn", p1=64, cols=64)),
               reads=("gat", "r_msq", "idn"), writes=("pS.1",))
            mark()
            dve(lambda e, rt, t4=t4: e.tensor_copy(
                    out=a32("rows", off=64 * t4, cols=64),
                    in_=pS[:, 512:576]),
                reads=(), writes=("pS.1",) + W32("rows") + ("r_ss.%d" % t4,))
        for t4 in range(4):
            sS = lambda t4=t4: a32("rows", off=64 * t4, cols=64)
            wA = lambda: a32("rows", off=256, cols=64)
            wB = lambda: a32("rows", off=320, cols=64)
            geb = lambda: a32("rows", off=384, cols=64)
            e0b = lambda: a32("rows", off=448, cols=64)
            m0 = lambda: a32("rows", off=512, cols=1)
            mx = lambda: a32("rows", off=513, cols=1)
            t16 = lambda: a32("rows", off=514, cols=1)
            zz = lambda: a32("rows", off=515, cols=1)
            dve(lambda e, rt, sS=sS: e.reduce_max(out=m0(), in_=sS(),
                                                  axis=AX.X, negate=True),
                reads=("r_ss.%d" % t4,), writes=("r_m0",))
            dve(lambda e, rt, sS=sS: e.tensor_copy(out=wA(), in_=sS()),
                reads=("r_ss.%d" % t4,), writes=("r_wa",))
            cur, nxt, ck, nk_ = wA, wB, "r_wa", "r_wb"
            for it in range(15):
                dve(lambda e, rt, cur=cur: e.reduce_max(out=mx(), in_=cur(),
                                                        axis=AX.X),
                    reads=(ck,), writes=("r_mx",))
                dve(lambda e, rt, cur=cur: e.tensor_scalar(
                        out=geb(), in0=cur(), scalar1=mx(), scalar2=None,
                        op0=ALU.is_ge),
                    reads=(ck, "r_mx"), writes=("r_ge",))
                dve(lambda e, rt, cur=cur, nxt=nxt: e.scalar_tensor_tensor(
                        out=nxt(), in0=geb(), scalar=NEG_BIG, in1=cur(),
                        op0=ALU.mult, op1=ALU.add),
                    reads=("r_ge", ck), writes=(nk_,))
                cur, nxt, ck, nk_ = nxt, cur, nk_, ck
            dve(lambda e, rt, cur=cur: e.reduce_max(out=t16(), in_=cur(),
                                                    axis=AX.X),
                reads=(ck,), writes=("r_t16",))
            act(lambda e, rt, sS=sS: e.activation(out=e0b(), in_=sS(),
                                                  func=AF.Exp, bias=m0()),
                reads=("r_ss.%d" % t4, "r_m0"), writes=("r_e0",))
            dve(lambda e, rt, sS=sS: e.tensor_scalar(
                    out=geb(), in0=sS(), scalar1=t16(), scalar2=None,
                    op0=ALU.is_ge),
                reads=("r_ss.%d" % t4, "r_t16"), writes=("r_ge",))
            dve(lambda e, rt: e.tensor_mul(out=e0b(), in0=e0b(), in1=geb()),
                reads=("r_e0", "r_ge"), writes=("r_e0",))
            dve(lambda e, rt: e.reduce_sum(out=zz(), in_=e0b(), axis=AX.X),
                reads=("r_e0",), writes=("r_zz",))
            dve(lambda e, rt: e.reciprocal(out=mx(), in_=zz()),
                reads=("r_zz",), writes=("r_mx",))
            dve(lambda e, rt: e.tensor_scalar(out=e0b(), in0=e0b(),
                                              scalar1=mx(), scalar2=None,
                                              op0=ALU.mult),
                reads=("r_e0", "r_mx"), writes=("r_e0",))
            pe(lambda e, rt: e.transpose(pS[0:64, 640:768], e0b(), a32("idn")),
               reads=("r_e0", "idn"), writes=("pS.1",))
            mark()
            dve(lambda e, rt, t4=t4: e.tensor_copy(
                    out=a16("gts", p1=64, off=128 * t4, cols=128),
                    in_=pS[0:64, 640:768]),
                reads=(), writes=("pS.1",) + W16("gts", t4))
            if BUILD_UPTO == 161 and t4 == 0:
                raise _StopPlan()
        if BUILD_UPTO == 162:
            raise _StopPlan()
        for j in range(8):
            pe(lambda e, rt, j=j: e.matmul(
                    pS[:, 1024:1536],
                    a16("repb", p1=64, off=128 * j, cols=128),
                    a16("gts", p1=64), start=True, stop=True),
               reads=("repb", "gts.0", "gts.1", "gts.2", "gts.3", "gts"),
               writes=("pS.2",))
            mark()
            dve(lambda e, rt, j=j: e.tensor_copy(
                    out=a16("gexp", off=512 * j, cols=512),
                    in_=pS[:, 1024:1536]),
                reads=(), writes=("pS.2",) + W16("gexp", j))
        for j in range(8):
            for hf in range(2):
                dma("sync", "d_pnl",
                    lambda e, rt, j=j, hf=hf: e.dma_start(
                        out=a32("pnl32", cols=2048).rearrange(
                            "p (c k) -> p c k", c=16),
                        in_=AAL.rearrange("(c p) n -> p c n", p=128)[
                            :, 16 * hf:16 * (hf + 1), 128 * j:128 * (j + 1)]),
                    writes=W32("pnl32"))
                pool(lambda e, rt, j=j, hf=hf: e.tensor_copy(
                        out=a16("pnl16", off=4096 * (j % 2) + 2048 * hf,
                                cols=2048),
                        in_=a32("pnl32", cols=2048)),
                     reads=Rk("pnl32"), writes=W16("pnl16", j % 2))
            for ch in range(DCH):
                pe(lambda e, rt, j=j, ch=ch: e.matmul(
                        pM[:, 512 * (j % 2):512 * (j % 2) + 512],
                        a16("pnl16", off=4096 * (j % 2) + 128 * ch, cols=128),
                        a16("yh", off=512 * ch, cols=512),
                        start=(ch == 0), stop=(ch == DCH - 1)),
                   reads=Rk("pnl16", j % 2) + Rk("yh", ch),
                   writes=("pM.%d" % (j % 2),))
            mark()
            if BUILD_UPTO == 16:
                dve(lambda e, rt, j=j: e.tensor_copy(
                        out=a16("hg", off=512 * j, cols=512),
                        in_=pM[:, 512 * (j % 2):512 * (j % 2) + 512]),
                    reads=(),
                    writes=("pM.%d" % (j % 2),) + W16("hg", j) + ("hraw.%d" % j,))
            else:
                dve(lambda e, rt, j=j: e.tensor_mul(
                        out=a16("hg", off=512 * j, cols=512),
                        in0=pM[:, 512 * (j % 2):512 * (j % 2) + 512],
                        in1=a16("gexp", off=512 * j, cols=512)),
                    reads=Rk("gexp", j),
                    writes=("pM.%d" % (j % 2),) + W16("hg", j))
            precast_some(precast_budget)
        if BUILD_UPTO == 16:
            for j in range(8):
                dve(lambda e, rt, j=j: e.tensor_copy(
                        out=a32("ostage", off=0, cols=512),
                        in_=a16("gexp", off=512 * j, cols=512)),
                    reads=Rk("gexp", j), writes=W32("ostage", 0))
                dma("gpsimd", "d_out",
                    lambda e, rt, j=j: e.dma_start(
                        out=OUT[128 * j:128 * (j + 1), 0:512],
                        in_=a32("ostage", off=0, cols=512)),
                    reads=Rk("ostage", 0), writes=("hbm_out",))
                dve(lambda e, rt, j=j: e.tensor_copy(
                        out=a32("ostage", off=512, cols=512),
                        in_=a16("hg", off=512 * j, cols=512)),
                    reads=("hraw.%d" % j,) + Rk("hg", j), writes=W32("ostage", 1))
                dma("gpsimd", "d_out",
                    lambda e, rt, j=j: e.dma_start(
                        out=OUT[1280 + 128 * j:1280 + 128 * (j + 1), 512:1024],
                        in_=a32("ostage", off=512, cols=512)),
                    reads=Rk("ostage", 1), writes=("hbm_out",))
            dve(lambda e, rt: e.tensor_copy(out=a32("ostage", off=0, cols=256),
                                            in_=a32("rows", off=0, cols=256)),
                reads=("r_ss.0", "r_ss.1", "r_ss.2", "r_ss.3"),
                writes=W32("ostage", 0))
            dma("gpsimd", "d_out",
                lambda e, rt: e.dma_start(out=OUT[1152:1280, 0:256],
                                          in_=a32("ostage", off=0, cols=256)),
                reads=Rk("ostage", 0), writes=("hbm_out",))
            prog.wait("gpsimd", ["hbm_out"])
            raise _StopPlan()
        for ii in range(16):
            dma("sync", "d_pnl",
                lambda e, rt, ii=ii: e.dma_start(
                    out=a32("pnl32", cols=2048).rearrange("p (r k) -> p r k", r=8),
                    in_=BAL.rearrange("(r p) n -> p r n", p=128)[
                        :, :, 256 * ii:256 * (ii + 1)]),
                writes=W32("pnl32"))
            pool(lambda e, rt, ii=ii: e.tensor_copy(
                    out=a16("pnl16", off=4096 * (ii % 2), cols=2048),
                    in_=a32("pnl32", cols=2048)),
                 reads=Rk("pnl32"), writes=W16("pnl16", ii % 2))
            for s2 in range(2):
                i = 2 * ii + s2
                for r in range(8):
                    pe(lambda e, rt, ii=ii, s2=s2, r=r, i=i: e.matmul(
                            pM[:, 512 * (i % 2):512 * (i % 2) + 512],
                            a16("pnl16", off=4096 * (ii % 2) + 256 * r
                                + 128 * s2, cols=128),
                            a16("hg", off=512 * r, cols=512),
                            start=(r == 0), stop=(r == 7)),
                       reads=Rk("pnl16", ii % 2) + Rk("hg", r),
                       writes=("pM.%d" % (i % 2),))
                mark()
                if final:
                    osl = i % 2
                    dve(lambda e, rt, i=i, hh=hh, osl=osl: e.tensor_tensor(
                            out=a32("ostage", off=512 * osl, cols=512),
                            in0=a16("xtb", off=T * i + 512 * hh, cols=512),
                            in1=pM[:, 512 * (i % 2):512 * (i % 2) + 512],
                            op=ALU.add),
                        reads=Rk("xtb", i),
                        writes=("pM.%d" % (i % 2),) + W32("ostage", osl))
                    dma("gpsimd", "d_out",
                        lambda e, rt, i=i, hh=hh, osl=osl: e.dma_start(
                            out=OUT[128 * i:128 * (i + 1),
                                    512 * hh:512 * (hh + 1)],
                            in_=a32("ostage", off=512 * osl, cols=512)),
                        reads=Rk("ostage", osl), writes=("hbm_out",))
                else:
                    dve(lambda e, rt, i=i, hh=hh: e.tensor_tensor(
                            out=a16("xtb", off=T * i + 512 * hh, cols=512),
                            in0=a16("xtb", off=T * i + 512 * hh, cols=512),
                            in1=pM[:, 512 * (i % 2):512 * (i % 2) + 512],
                            op=ALU.add),
                        reads=(),
                        writes=("pM.%d" % (i % 2),) + W16("xtb", i))
                    pool(lambda e, rt, i=i, hh=hh, s2=s2: e.tensor_mul(
                            out=a16("xsq", off=512 * s2, cols=512),
                            in0=a16("xtb", off=T * i + 512 * hh, cols=512),
                            in1=a16("xtb", off=T * i + 512 * hh, cols=512)),
                         reads=Rk("xtb", i), writes=W16("xsq", s2))
                    pe(lambda e, rt, i=i, hh=hh, ps=ps_sum: e.matmul(
                            ps, a16("onesb", cols=1),
                            a16("xtb", off=T * i + 512 * hh, cols=512),
                            start=(i == 0), stop=(i == DCH - 1)),
                       reads=Rk("xtb", i) + ("onesb",), writes=(k_sum,))
                    pe(lambda e, rt, i=i, s2=s2, ps=ps_sq: e.matmul(
                            ps, a16("onesb", cols=1),
                            a16("xsq", off=512 * s2, cols=512),
                            start=(i == 0), stop=(i == DCH - 1)),
                       reads=Rk("xsq", s2) + ("onesb",), writes=(k_sq,))
                    mark()
            precast_some(precast_budget)

    def barrier(bidx, payload_reads):
        dma_pair("sync", "d_flg",
                 lambda e, rt, bidx=bidx: e.dma_start(
                     out=SHF[bidx, 0][0:1, 0:1],
                     in_=SNON[0:1, bidx:bidx + 1], cond=rt["iseven"]),
                 lambda e, rt, bidx=bidx: e.dma_start(
                     out=SHF[bidx, 1][0:1, 0:1],
                     in_=SNON[0:1, bidx:bidx + 1], cond=rt["isodd"]),
                 reads=tuple(payload_reads) + ("snon",),
                 writes=("flag%d" % bidx,))

        def pollfn(e, rt, bidx=bidx):
            nval = e.value_load(SNON[0:1, bidx:bidx + 1])
            fv = SHF[bidx].rearrange("a b c -> b (a c)")

            def cond():
                e.sem_clear(flpoll)
                e.dma_start(out=SFLG, in_=fv).then_inc(flpoll, 16)
                e.wait_ge(flpoll, 16)
                va = e.value_load(SFLG[0:1, 0:1])
                vb = e.value_load(SFLG[0:1, 2:3])
                return ((va == nval) & (vb == nval)) == 0

            with e.While(cond):
                e.nop()
            return e.nop()

        prog.op("sync", pollfn, reads=("flag%d" % bidx, "snon"),
                writes=("xbar%d" % bidx,), sem=prog.sem("bar%d" % bidx), amt=1)

    def emit_dump():
        for ch in range(DCH):
            for tt in range(2):
                sl = tt
                dve(lambda e, rt, ch=ch, tt=tt, sl=sl: e.tensor_copy(
                        out=a32("ostage", off=512 * sl, cols=512),
                        in_=a16("xtb", off=T * ch + 512 * tt, cols=512)),
                    reads=Rk("xtb", ch), writes=W32("ostage", sl))
                dma("gpsimd", "d_out",
                    lambda e, rt, ch=ch, tt=tt, sl=sl: e.dma_start(
                        out=OUT[128 * ch:128 * (ch + 1),
                                512 * tt:512 * (tt + 1)],
                        in_=a32("ostage", off=512 * sl, cols=512)),
                    reads=Rk("ostage", sl), writes=("hbm_out",))
        prog.wait("gpsimd", ["hbm_out"])

    def emit_dump_yh():
        for ch in range(DCH):
            dve(lambda e, rt, ch=ch: e.tensor_copy(
                    out=a32("ostage", off=0, cols=512),
                    in_=a16("yh", off=512 * ch, cols=512)),
                reads=Rk("yh", ch), writes=W32("ostage", 0))
            dma("gpsimd", "d_out",
                lambda e, rt, ch=ch: e.dma_start(
                    out=OUT[128 * ch:128 * (ch + 1), 0:512],
                    in_=a32("ostage", off=0, cols=512)),
                reads=Rk("ostage", 0), writes=("hbm_out",))
        prog.wait("gpsimd", ["hbm_out"])

    def emit_attention():
        for h in range(NH):
            hp = h % 2
            for p3 in range(3):
                psl = (3 * h + p3) % 2
                dma("sync", "d_qp%d" % psl,
                    lambda e, rt, p3=p3, h=h, psl=psl: e.dma_start(
                        out=a16("qkvp", off=4096 * psl, cols=4096),
                        in_=WQKVBF[p3, h]),
                    reads=("wbf.%d.%d" % (p3, h),), writes=W16("qkvp", psl))
                for tt in range(4):
                    for ch in range(DCH):
                        pe(lambda e, rt, tt=tt, ch=ch, psl=psl: e.matmul(
                                pM[:, 512 * (tt % 2):512 * (tt % 2) + 512],
                                a16("qkvp", off=4096 * psl + 128 * ch, cols=128),
                                a16("y2f", off=2048 * ch + 512 * tt, cols=512),
                                start=(ch == 0), stop=(ch == DCH - 1)),
                           reads=Rk("qkvp", psl) + ("y2f.0", "y2f.1", "y2f"),
                           writes=("pM.%d" % (tt % 2),))
                    mark()
                    if p3 == 0:
                        act(lambda e, rt, tt=tt, hp=hp: e.activation(
                                out=a16("qbuf", off=2048 * hp + 512 * tt,
                                        cols=512),
                                in_=pM[:, 512 * (tt % 2):512 * (tt % 2) + 512],
                                func=AF.Identity, scale=SCALE_QK),
                            reads=(),
                            writes=("pM.%d" % (tt % 2),) + W16("qbuf", hp))
                    elif p3 == 1:
                        act(lambda e, rt, tt=tt, hp=hp: e.copy(
                                out=a16("kbuf", off=2048 * hp + 512 * tt,
                                        cols=512),
                                in_=pM[:, 512 * (tt % 2):512 * (tt % 2) + 512]),
                            reads=(),
                            writes=("pM.%d" % (tt % 2),) + W16("kbuf", hp))
                    else:
                        act(lambda e, rt, tt=tt: e.copy(
                                out=a16("vtmp"),
                                in_=pM[:, 512 * (tt % 2):512 * (tt % 2) + 512]),
                            reads=(),
                            writes=("pM.%d" % (tt % 2),) + W16("vtmp"))
                        for s in range(4):
                            dma("sync", "d_vt",
                                lambda e, rt, tt=tt, s=s, hp=hp:
                                e.dma_start_transpose(
                                    out=a16("vtok", off=2048 * hp
                                            + 128 * (4 * tt + s), cols=128),
                                    in_=a16("vtmp", off=128 * s, cols=128)),
                                reads=("vtmp",), writes=W16("vtok", hp))
            for bname, dst_off in (("qbuf", RZ_A), ("kbuf", RZ_B)):
                for piece in range(4):
                    pool(lambda e, rt, bname=bname, piece=piece, hp=hp:
                         e.tensor_mul(
                            out=a16("usq"),
                            in0=a16(bname, off=2048 * hp + 512 * piece, cols=512),
                            in1=a16(bname, off=2048 * hp + 512 * piece, cols=512)),
                         reads=Rk(bname, hp), writes=W16("usq"))
                    pe(lambda e, rt: e.matmul(pU, a16("onesb", cols=1),
                                              a16("usq"), start=True, stop=True),
                       reads=("usq", "onesb"), writes=("pU",))
                    mark()
                    dve(lambda e, rt: e.tensor_copy(
                            out=a32("rows", p1=1, off=0, cols=512), in_=pU),
                        reads=(), writes=("pU", "r_urow") + W32("rows"))
                    dve(lambda e, rt: e.reduce_max(
                            out=a32("rz", p1=1, off=RZ_C, cols=1),
                            in_=a32("rows", p1=1, off=0, cols=512), axis=AX.X),
                        reads=("r_urow",), writes=("r_uc",))
                    if piece == 0:
                        dve(lambda e, rt, dst_off=dst_off: e.tensor_copy(
                                out=a32("rz", p1=1, off=dst_off, cols=1),
                                in_=a32("rz", p1=1, off=RZ_C, cols=1)),
                            reads=("r_uc",), writes=("r_n2.%d" % dst_off,))
                    else:
                        dve(lambda e, rt, dst_off=dst_off: e.tensor_tensor(
                                out=a32("rz", p1=1, off=dst_off, cols=1),
                                in0=a32("rz", p1=1, off=dst_off, cols=1),
                                in1=a32("rz", p1=1, off=RZ_C, cols=1),
                                op=ALU.max),
                            reads=("r_uc", "r_n2.%d" % dst_off),
                            writes=("r_n2.%d" % dst_off,))
            dve(lambda e, rt: e.tensor_mul(
                    out=a32("rz", p1=1, off=RZ_C, cols=1),
                    in0=a32("rz", p1=1, off=RZ_A, cols=1),
                    in1=a32("rz", p1=1, off=RZ_B, cols=1)),
                reads=("r_n2.%d" % RZ_A, "r_n2.%d" % RZ_B), writes=("r_uc",))
            act(lambda e, rt: e.activation(
                    out=a32("rz", p1=1, off=RZ_A, cols=1),
                    in_=a32("rz", p1=1, off=RZ_C, cols=1), func=AF.Ln),
                reads=("r_uc",), writes=("r_n2.%d" % RZ_A,))
            act(lambda e, rt: e.activation(
                    out=a32("rz", p1=1, off=RZ_C, cols=1),
                    in_=a32("rz", p1=1, off=RZ_A, cols=1),
                    func=AF.Exp, scale=0.5),
                reads=("r_n2.%d" % RZ_A,), writes=("r_uc",))
            dve(lambda e, rt: e.tensor_scalar(
                    out=a32("rz", p1=1, off=RZ_A, cols=1),
                    in0=a32("rz", p1=1, off=RZ_C, cols=1), scalar1=-1.0,
                    scalar2=None, op0=ALU.mult),
                reads=("r_uc",), writes=("r_n2.%d" % RZ_A,))
            pe(lambda e, rt: e.matmul(pM[:, 0:1],
                                      a32("onesrow", p1=1, cols=128),
                                      a32("rz", p1=1, off=RZ_A, cols=1),
                                      start=True, stop=True),
               reads=("r_n2.%d" % RZ_A, "onesrow"), writes=("pM.0",))
            mark()
            dve(lambda e, rt: e.tensor_copy(
                    out=a32("rz", off=RZ_NEGU, cols=1), in_=pM[:, 0:1]),
                reads=(), writes=("pM.0", "r_negub"))

            for J in range(4):
                ngr = 2 * J + 2
                nk = 4 * J + 4

                def s_group(g, J=J, hp=hp):
                    bk = 2 * (g % 2)
                    for s2 in range(2):
                        i = 2 * g + s2
                        pe(lambda e, rt, i=i, s2=s2, bk=bk, J=J, hp=hp:
                           e.matmul(
                               pS[:, 512 * (bk + s2):512 * (bk + s2) + 512],
                               a16("kbuf", off=2048 * hp + 128 * i, cols=128),
                               a16("qbuf", off=2048 * hp + 512 * J, cols=512),
                               start=True, stop=True),
                           reads=Rk("kbuf", hp) + Rk("qbuf", hp),
                           writes=("pS.%d" % (bk + s2),))
                    mark()

                def exp_group(g, J=J):
                    bk = 2 * (g % 2)
                    esl = g % 2
                    act(lambda e, rt, bk=bk, esl=esl: e.activation(
                            out=a16("ebuf", off=1024 * esl, cols=1024),
                            in_=pS[:, 512 * bk:512 * bk + 1024],
                            func=AF.Exp,
                            bias=a32("rz", off=RZ_NEGU, cols=1)),
                        reads=("r_negub",),
                        writes=("pS.%d" % bk, "pS.%d" % (bk + 1))
                        + W16("ebuf", esl))
                    if g >= 2 * J:
                        for s2 in range(2):
                            a = 2 * g + s2 - 4 * J
                            dve(lambda e, rt, s2=s2, a=a, esl=esl:
                                e.tensor_mul(
                                    out=a16("ebuf", off=1024 * esl + 512 * s2,
                                            cols=512),
                                    in0=a16("ebuf", off=1024 * esl + 512 * s2,
                                            cols=512),
                                    in1=a16("maskb", off=384 - 128 * a,
                                            cols=512)),
                                reads=("maskb", "ebuf.%d" % esl),
                                writes=("ebuf.%d" % esl,))

                def zpv_group(g, J=J, nk=nk, hp=hp):
                    esl = g % 2
                    for s2 in range(2):
                        i = 2 * g + s2
                        pe(lambda e, rt, i=i, s2=s2, esl=esl, nk=nk: e.matmul(
                                pZ, a16("onesb", cols=1),
                                a16("ebuf", off=1024 * esl + 512 * s2,
                                    cols=512),
                                start=(i == 0), stop=(i == nk - 1)),
                           reads=("ebuf.%d" % esl, "onesb"), writes=("pZ",))
                        pe(lambda e, rt, i=i, s2=s2, esl=esl, nk=nk, hp=hp:
                           e.matmul(
                               pM[:, 0:512],
                               a16("vtok", off=2048 * hp + 128 * i, cols=128),
                               a16("ebuf", off=1024 * esl + 512 * s2, cols=512),
                               start=(i == 0), stop=(i == nk - 1)),
                           reads=("ebuf.%d" % esl,) + Rk("vtok", hp),
                           writes=("pM.0",))
                    mark()

                s_group(0)
                if ngr > 1:
                    exp_group(0)
                    s_group(1)
                    for g in range(2, ngr):
                        zpv_group(g - 2)
                        exp_group(g - 1)
                        s_group(g)
                    zpv_group(ngr - 2)
                    exp_group(ngr - 1)
                    zpv_group(ngr - 1)
                else:
                    exp_group(0)
                    zpv_group(0)

                dve(lambda e, rt: e.tensor_copy(
                        out=a32("rows", p1=1, off=512, cols=512), in_=pZ),
                    reads=(), writes=("pZ", "r_zrow") + W32("rows"))
                dve(lambda e, rt: e.reciprocal(
                        out=a32("rows", p1=1, off=0, cols=512),
                        in_=a32("rows", p1=1, off=512, cols=512)),
                    reads=("r_zrow",), writes=("r_zrec",))
                pe(lambda e, rt: e.matmul(pM[:, 512:1024],
                                          a32("onesrow", p1=1, cols=128),
                                          a32("rows", p1=1, off=0, cols=512),
                                          start=True, stop=True),
                   reads=("r_zrec", "onesrow"), writes=("pM.1",))
                mark()
                dve(lambda e, rt: e.tensor_copy(out=a32("rz", cols=512),
                                                in_=pM[:, 512:1024]),
                    reads=(), writes=("pM.1", "r_rzs"))
                osl = J % 2
                dve(lambda e, rt, osl=osl: e.tensor_mul(
                        out=a16("ostg", off=512 * osl, cols=512),
                        in0=a32("rz", cols=512),
                        in1=pM[:, 0:512]),
                    reads=("r_rzs",),
                    writes=("pM.0",) + W16("ostg", osl))
                dma("gpsimd", "d_ol",
                    lambda e, rt, h=h, J=J, osl=osl: e.dma_start(
                        out=OL[h][:, 512 * J:512 * (J + 1)],
                        in_=a16("ostg", off=512 * osl, cols=512)),
                    reads=Rk("ostg", osl), writes=("ol",))
            precast_some(3)
            if BUILD_UPTO == 41 and h == 0:
                for nm, row0 in (("qbuf", 0), ("kbuf", 128), ("vtok", 256)):
                    for piece in range(4):
                        dve(lambda e, rt, nm=nm, piece=piece: e.tensor_copy(
                                out=a32("ostage", off=0, cols=512),
                                in_=a16(nm, off=512 * piece, cols=512)),
                            reads=(nm, nm + ".0"), writes=W32("ostage", 0))
                        dma("gpsimd", "d_out",
                            lambda e, rt, nm=nm, row0=row0, piece=piece:
                            e.dma_start(
                                out=OUT[row0:row0 + 128,
                                        512 * (piece % 2):512 * (piece % 2) + 512]
                                if False else
                                OUT[row0 + (piece // 2) * 512:row0 + (piece // 2) * 512 + 128,
                                    512 * (piece % 2):512 * (piece % 2) + 512],
                                in_=a32("ostage", off=0, cols=512)),
                            reads=Rk("ostage", 0), writes=("hbm_out",))
                for piece in range(4):
                    dma("sync", "d_dbg",
                        lambda e, rt, piece=piece: e.dma_start(
                            out=a16("ebuf", off=0, cols=512),
                            in_=OL[0][:, 512 * piece:512 * (piece + 1)]),
                        reads=("ol",), writes=W16("ebuf", 0))
                    dve(lambda e, rt: e.tensor_copy(
                            out=a32("ostage", off=512, cols=512),
                            in_=a16("ebuf", off=0, cols=512)),
                        reads=("ebuf.0",), writes=W32("ostage", 1))
                    dma("gpsimd", "d_out",
                        lambda e, rt, piece=piece: e.dma_start(
                            out=OUT[384 + (piece // 2) * 512:384 + (piece // 2) * 512 + 128,
                                    512 * (piece % 2):512 * (piece % 2) + 512],
                            in_=a32("ostage", off=512, cols=512)),
                        reads=Rk("ostage", 1), writes=("hbm_out",))
                for piece in range(4):
                    dve(lambda e, rt, piece=piece: e.tensor_copy(
                            out=a32("ostage", off=512, cols=512),
                            in_=a16("y2f", off=512 * piece, cols=512)),
                        reads=("y2f", "y2f.0", "y2f.1"), writes=W32("ostage", 1))
                    dma("gpsimd", "d_out",
                        lambda e, rt, piece=piece: e.dma_start(
                            out=OUT[1100 + (piece // 2) * 512:1100 + (piece // 2) * 512 + 128,
                                    512 * (piece % 2):512 * (piece % 2) + 512],
                            in_=a32("ostage", off=512, cols=512)),
                        reads=Rk("ostage", 1), writes=("hbm_out",))
                prog.wait("gpsimd", ["hbm_out"])
                raise _StopPlan()

    class _StopPlan(Exception):
        pass

    try:
        if BUILD_UPTO < 2:
            emit_dump()
            raise _StopPlan()
        ln_finalize(pZ, "pZ", pU, "pU", slot=0)
        ln_finalize(pS[0:1, 0:512], "pS.0", pS[0:1, 512:1024], "pS.1", slot=1)
        for hh in range(2):
            for ch in range(DCH):
                ln_apply(ch, hh, hh,
                         lambda ch=ch: a16("yh", off=512 * ch, cols=512),
                         W16("yh", ch))
            if BUILD_UPTO == 15:
                emit_dump_yh()
                raise _StopPlan()
            emit_ffn(hh, final=False, k_sum="pZ", k_sq="pU", ps_sum=pZ, ps_sq=pU,
                     precast_budget=2)
            if BUILD_UPTO == 43:
                dve(lambda e, rt: e.tensor_copy(
                        out=a32("ostage", p1=1, off=0, cols=512), in_=pZ),
                    reads=(), writes=("pZ",) + W32("ostage", 0))
                dve(lambda e, rt: e.tensor_copy(
                        out=a32("ostage", p1=1, off=512, cols=512), in_=pU),
                    reads=(), writes=("pU",) + W32("ostage", 1))
                dma("gpsimd", "d_out",
                    lambda e, rt: e.dma_start(out=OUT[0:1, 0:512],
                                              in_=a32("ostage", p1=1, off=0, cols=512)),
                    reads=Rk("ostage", 0), writes=("hbm_out",))
                dma("gpsimd", "d_out",
                    lambda e, rt: e.dma_start(out=OUT[1:2, 0:512],
                                              in_=a32("ostage", p1=1, off=512, cols=512)),
                    reads=Rk("ostage", 1), writes=("hbm_out",))
                prog.wait("gpsimd", ["hbm_out"])
                raise _StopPlan()
            if BUILD_UPTO == 17:
                emit_dump()
                raise _StopPlan()
            ln_finalize(pZ, "pZ", pU, "pU", slot=0)
            for ch in range(DCH):
                sl = ch % 2
                ln_apply(ch, hh, 0,
                         lambda ch=ch, sl=sl: a16("y2stg", off=1024 * sl, cols=512),
                         W16("y2stg", sl))
                dma("gpsimd", "d_y2l",
                    lambda e, rt, ch=ch, hh=hh, sl=sl: e.dma_start(
                        out=Y2L[ch][:, 512 * hh:512 * (hh + 1)],
                        in_=a16("y2stg", off=1024 * sl, cols=512)),
                    reads=Rk("y2stg", sl), writes=("y2l",))

        if BUILD_UPTO < 3:
            emit_dump()
        else:
            dma("gpsimd", "d_xsp",
                lambda e, rt: e.dma_start(
                    out=XSP.rearrange("c p t -> p c t"),
                    in_=a16("xtb").rearrange("p (c t) -> p c t", c=DCH)),
                reads=tuple("xtb.%d" % c for c in range(DCH)) + ("xtb",),
                writes=("hbm_xsp",))
            dma_pair("scalar", "d_y2c",
                     lambda e, rt: e.dma_start(out=SHY[0], in_=Y2L,
                                               cond=rt["iseven"]),
                     lambda e, rt: e.dma_start(out=SHY[1], in_=Y2L,
                                               cond=rt["isodd"]),
                     reads=("y2l",), writes=("shy",))
            barrier(0, ["shy"])
            for sl in range(2):
                dma("sync", "d_y2r",
                    lambda e, rt, sl=sl: e.dma_start(
                        out=a16("y2f").rearrange("p (c t) -> p c t", c=DCH)[
                            :, :, 1024 * sl:1024 * (sl + 1)],
                        in_=SHY[sl].rearrange("c p t -> p c t")),
                    reads=("xbar0",), writes=W16("y2f", sl))

            emit_attention()
            dma_pair("scalar", "d_oc",
                     lambda e, rt: e.dma_start(out=OSH[0], in_=OL,
                                               cond=rt["iseven"]),
                     lambda e, rt: e.dma_start(out=OSH[1], in_=OL,
                                               cond=rt["isodd"]),
                     reads=("ol",), writes=("osh",))
            barrier(1, ["osh"])

            dma("sync", "d_xtl",
                lambda e, rt: e.dma_start(
                    out=a16("xtb").rearrange("p (c t) -> p c t", c=DCH),
                    in_=XSP.rearrange("c p t -> p c t")),
                reads=("hbm_xsp", "xbar1"),
                writes=W16("xtb") + tuple("xtb.%d" % c for c in range(DCH)))
            for sl in range(2):
                dma_pair("scalar", "d_ore",
                         lambda e, rt, sl=sl: e.dma_start(
                             out=a16("ores").rearrange("p (c t) -> p c t", c=DCH)[
                                 :, 16 * sl:16 * (sl + 1), :],
                             in_=OSH[sl][:, :, 0:T].rearrange("c p t -> p c t"),
                             cond=rt["iseven"]),
                         lambda e, rt, sl=sl: e.dma_start(
                             out=a16("ores").rearrange("p (c t) -> p c t", c=DCH)[
                                 :, 16 * sl:16 * (sl + 1), :],
                             in_=OSH[sl][:, :, T:S].rearrange("c p t -> p c t"),
                             cond=rt["isodd"]),
                         reads=("xbar1",), writes=W16("ores", sl))

            if BUILD_UPTO == 45:

                for vd in range(DCH):

                    for tt in range(2):

                        dve(lambda e, rt, vd=vd, tt=tt: e.tensor_copy(

                                out=a32("ostage", off=512 * tt, cols=512),

                                in_=a16("ores", off=T * vd + 512 * tt, cols=512)),

                            reads=("ores", "ores.0", "ores.1"), writes=W32("ostage", tt))

                        dma("gpsimd", "d_out",

                            lambda e, rt, vd=vd, tt=tt: e.dma_start(

                                out=OUT[128 * vd:128 * (vd + 1), 512 * tt:512 * (tt + 1)],

                                in_=a32("ostage", off=512 * tt, cols=512)),

                            reads=Rk("ostage", tt), writes=("hbm_out",))

                prog.wait("gpsimd", ["hbm_out"])

                raise _StopPlan()

            for ii in range(DCH):
                for hf in range(2):
                    dma("sync", "d_pnl",
                        lambda e, rt, ii=ii, hf=hf: e.dma_start(
                            out=a32("pnl32", cols=2048).rearrange(
                                "p (c k) -> p c k", c=16),
                            in_=WO.rearrange("(c p) n -> p c n", p=128)[
                                :, 16 * hf:16 * (hf + 1), 128 * ii:128 * (ii + 1)]),
                        writes=W32("pnl32"))
                    pool(lambda e, rt, ii=ii, hf=hf: e.tensor_copy(
                            out=a16("pnl16", off=4096 * (ii % 2) + 2048 * hf,
                                    cols=2048),
                            in_=a32("pnl32", cols=2048)),
                         reads=Rk("pnl32"), writes=W16("pnl16", ii % 2))
                for tt in range(2):
                    for vd in range(DCH):
                        pe(lambda e, rt, ii=ii, tt=tt, vd=vd: e.matmul(
                                pM[:, 512 * tt:512 * (tt + 1)],
                                a16("pnl16", off=4096 * (ii % 2) + 128 * vd,
                                    cols=128),
                                a16("ores", off=T * vd + 512 * tt, cols=512),
                                start=(vd == 0), stop=(vd == DCH - 1)),
                           reads=Rk("pnl16", ii % 2) + Rk("ores", vd // 16),
                           writes=("pM.%d" % tt,))
                    mark()
                    dve(lambda e, rt, ii=ii, tt=tt: e.tensor_tensor(
                            out=a16("xtb", off=T * ii + 512 * tt, cols=512),
                            in0=a16("xtb", off=T * ii + 512 * tt, cols=512),
                            in1=pM[:, 512 * tt:512 * (tt + 1)],
                            op=ALU.add),
                        reads=(),
                        writes=("pM.%d" % tt,) + W16("xtb", ii))
                    if BUILD_UPTO >= 7:
                        pool(lambda e, rt, ii=ii, tt=tt: e.tensor_mul(
                                out=a16("xsq", off=512 * tt, cols=512),
                                in0=a16("xtb", off=T * ii + 512 * tt, cols=512),
                                in1=a16("xtb", off=T * ii + 512 * tt, cols=512)),
                             reads=Rk("xtb", ii), writes=W16("xsq", tt))
                        pe(lambda e, rt, ii=ii, tt=tt: e.matmul(
                                pS[0:1, 1024 * tt:1024 * tt + 512],
                                a16("onesb", cols=1),
                                a16("xtb", off=T * ii + 512 * tt, cols=512),
                                start=(ii == 0), stop=(ii == DCH - 1)),
                           reads=Rk("xtb", ii) + ("onesb",),
                           writes=("pS.%d" % (2 * tt),))
                        pe(lambda e, rt, ii=ii, tt=tt: e.matmul(
                                pS[0:1, 1024 * tt + 512:1024 * tt + 1024],
                                a16("onesb", cols=1),
                                a16("xsq", off=512 * tt, cols=512),
                                start=(ii == 0), stop=(ii == DCH - 1)),
                           reads=Rk("xsq", tt) + ("onesb",),
                           writes=("pS.%d" % (2 * tt + 1),))
                        mark()

            if BUILD_UPTO < 7:
                emit_dump()
            else:
                dma("sync", "d_x0",
                    lambda e, rt: e.dma_start(
                        out=a32("xstage", cols=2048).rearrange(
                            "p (c k) -> p c k", c=DCH),
                        in_=WRT.rearrange("(c p) k -> p c k", p=128)),
                    writes=W32("xstage", 0) + W32("xstage", 1))
                pool(lambda e, rt: e.tensor_copy(out=a16("wrtb"),
                                                 in_=a32("xstage", cols=2048)),
                     reads=Rk("xstage", 0) + Rk("xstage", 1), writes=W16("wrtb"))
                dma("sync", "d_cst",
                    lambda e, rt: e.dma_start(out=a16("repb", p1=64), in_=REP),
                    writes=W16("repb"))
                ln_finalize(pS[0:1, 0:512], "pS.0", pS[0:1, 512:1024], "pS.1",
                            slot=0)
                ln_finalize(pS[0:1, 1024:1536], "pS.2", pS[0:1, 1536:2048], "pS.3",
                            slot=1)
                for hh in range(2):
                    for ch in range(DCH):
                        ln_apply(ch, hh, hh,
                                 lambda ch=ch: a16("yh", off=512 * ch, cols=512),
                                 W16("yh", ch))
                    emit_ffn(hh, final=True)
                prog.wait("gpsimd", ["hbm_out"])
                barrier(2, ["ores", "xtb"])
    except _StopPlan:
        pass


    for semname, q in dmaq.items():
        if prog.cnt.get(semname, 0) > 0:
            prog.ops[q].append(dict(waits={semname: prog.cnt[semname]},
                                    fn=None, sem=None, amt=0))

    semmap = {}
    for sname in prog.semnames:
        semmap[sname] = est.enter_context(nc.semaphore("s_" + sname))

    def replay(engname, eng):
        rt = {}
        pid = eng.partition_id()
        par = pid % 2
        rt["iseven"] = par == 0
        rt["isodd"] = par == 1
        for rec in prog.ops[engname]:
            for s, v in rec["waits"].items():
                eng.wait_ge(semmap[s], v)
            if rec["fn"] is not None:
                r = rec["fn"](eng, rt)
                if rec["sem"] is not None:
                    hdl = semmap[rec["sem"]]
                    if isinstance(r, list):
                        each = rec["amt"] // len(r)
                        for ins in r:
                            ins.then_inc(hdl, each)
                    else:
                        r.then_inc(hdl, rec["amt"])

    with nc.Block() as block:
        @block.sync
        def _(sync):
            replay("sync", sync)

        @block.tensor
        def _(tensor):
            replay("tensor", tensor)

        @block.vector
        def _(vector):
            replay("vector", vector)

        @block.scalar
        def _(scalar):
            replay("scalar", scalar)

        @block.gpsimd
        def _(gpsimd):
            replay("gpsimd", gpsimd)

    est.close()
    return nc


_NC_CACHE = [None]


def _get_nc():
    if _NC_CACHE[0] is None:
        _NC_CACHE[0] = build_program()
    return _NC_CACHE[0]


def kernel(L0_attn_hidden_states, L0_attn_wq, L0_attn_wkv, L0_attn_wo,
           L0_ffn_A, L0_ffn_B, L0_ffn_Wr):
    x = np.asarray(L0_attn_hidden_states, dtype=np.float32)
    wq = np.asarray(L0_attn_wq, dtype=np.float32)
    wkv = np.asarray(L0_attn_wkv, dtype=np.float32)
    wo = np.ascontiguousarray(np.asarray(L0_attn_wo, dtype=np.float32))
    A = np.asarray(L0_ffn_A, dtype=np.float32)
    Bm = np.asarray(L0_ffn_B, dtype=np.float32)
    Wr = np.asarray(L0_ffn_Wr, dtype=np.float32)

    wq_used = wq.reshape(D, 32, 192)[:, :, :128]
    wk_full = wkv[:, :4096].reshape(D, 32, 128)
    wv_full = wkv[:, 4096:8192].reshape(D, 32, 128)
    aal = np.ascontiguousarray(A.transpose(2, 0, 1).reshape(D, KR))
    bal = np.ascontiguousarray(Bm.transpose(0, 2, 1).reshape(KR, D))
    wrt = np.ascontiguousarray(Wr.T)

    idn32 = np.eye(128, dtype=np.float32)
    onesrow = np.ones((1, 128), dtype=np.float32)
    onesrb = np.ones((1, 128), dtype=ml_dtypes.bfloat16)
    onesb = np.ones((128, 2), dtype=ml_dtypes.bfloat16)
    kk = np.arange(128)[:, None]
    uu = np.arange(896)[None, :]
    maskb = (uu >= kk + 384).astype(ml_dtypes.bfloat16)
    rep = (np.arange(KR)[None, :] // RANK ==
           np.arange(64)[:, None]).astype(ml_dtypes.bfloat16)
    nonce = np.random.randint(1, 2 ** 31 - 1, size=(1, 8)).astype(np.int32)

    in_maps = []
    for c in range(8):
        b, half = divmod(c, 2)
        hs = slice(NH * half, NH * (half + 1))
        xt = np.ascontiguousarray(x[b, T * half:T * (half + 1), :].T)
        wqkv = np.ascontiguousarray(np.stack([
            wq_used[:, hs, :].reshape(D, NH * 128),
            wk_full[:, hs, :].reshape(D, NH * 128),
            wv_full[:, hs, :].reshape(D, NH * 128)]))
        in_maps.append(dict(
            xt=xt, wrt=wrt, aal=aal, bal=bal, wqkv=wqkv, wo=wo,
            idn32=idn32, onesrow=onesrow, onesrb=onesrb, onesb=onesb,
            maskb=maskb, rep=rep, nonce=nonce))

    nc = _get_nc()
    res = run_bass_kernel_spmd(nc, in_maps, core_ids=list(range(8)))
    out = np.empty((B, S, D), dtype=np.float32)
    for c in range(8):
        b, half = divmod(c, 2)
        out[b, T * half:T * (half + 1), :] = res.results[c]["out"].T
    return out


# version 03
# speedup vs baseline: 1.2985x; 1.2985x over previous
import numpy as np
import ml_dtypes
import concourse.bass as bass
import concourse.mybir as mybir
from concourse.bass_utils import run_bass_kernel_spmd

F32 = mybir.dt.float32
BF16 = mybir.dt.bfloat16
I32 = mybir.dt.int32
AF = mybir.ActivationFunctionType
ALU = mybir.AluOpType
AX = mybir.AxisListType

B, S, D = 4, 2048, 4096
NH = 16
T = 1024
DCH = 32
KEXP, RANK = 64, 16
KR = KEXP * RANK
SCALE_QK = float(192.0 ** -0.5)
EPS = 1e-5
NEG_BIG = -1.0e30
INV_D = 1.0 / D

BUILD_UPTO = 99
ENGINES = ("sync", "tensor", "vector", "scalar", "gpsimd")


class Arena:
    def __init__(self, total):
        self.total = total
        self.bufs = {}

    def buf(self, name, off, cols):
        assert off + cols <= self.total, (name, off, cols, self.total)
        self.bufs[name] = (off, cols)

    def overlaps(self, name):
        o, c = self.bufs[name]
        return [n2 for n2, (o2, c2) in self.bufs.items()
                if n2 != name and o < o2 + c2 and o2 < o + c]


class Pr:

    def __init__(self):
        self.ops = {e: [] for e in ENGINES}
        self.cnt = {}
        self.semnames = []
        self.last_w = {}
        self.readers = {}
        self.hw = {e: {} for e in ENGINES}
        self.pe_pend = []

    def sem(self, name):
        if name not in self.cnt:
            self.cnt[name] = 0
            self.semnames.append(name)
        return name

    def _need(self, eng, tag, need):
        if tag is None:
            return
        if isinstance(tag, list):
            for t in tag:
                self._need(eng, t, need)
            return
        if tag[0] == 'PEPEND':
            assert eng == 'tensor', "consumer needs un-marked PE result"
            return
        s, v = tag
        if eng == 'tensor' and s == 'pe':
            return
        if self.hw[eng].get(s, 0) >= v:
            return
        need[s] = max(need.get(s, 0), v)

    @staticmethod
    def _k(key):
        return "ROWS" if key.startswith("r_") else key

    def op(self, eng, fn, reads=(), writes=(), sem=None, amt=0):
        reads = tuple(dict.fromkeys(self._k(k) for k in reads))
        writes = tuple(dict.fromkeys(self._k(k) for k in writes))
        need = {}
        for r in reads:
            self._need(eng, self.last_w.get(r), need)
        for w in writes:
            self._need(eng, self.last_w.get(w), need)
            for t in self.readers.get(w, ()):
                self._need(eng, t, need)
        for s, v in need.items():
            self.hw[eng][s] = max(self.hw[eng].get(s, 0), v)

        if sem is None and eng in ("vector", "scalar", "gpsimd") and fn is not None:
            sem = {"vector": "dve", "scalar": "act", "gpsimd": "pool"}[eng]
            amt = 1
        tag = None
        if sem is not None:
            self.sem(sem)
            self.cnt[sem] += amt
            tag = (sem, self.cnt[sem])

        self.ops[eng].append(dict(waits=dict(need), fn=fn, sem=sem, amt=amt))

        if eng == "tensor" and tag is None and fn is not None:
            for r in reads:
                lst = self.readers.setdefault(r, [])
                self.pe_pend.append(('r', r, len(lst)))
                lst.append(('PEPEND', 0))
            for w in writes:
                self.pe_pend.append(('w', w, -1))
                self.last_w[w] = ('PEPEND', 0)
                self.readers[w] = []
        elif tag is not None:
            for r in reads:
                self.readers.setdefault(r, []).append(tag)
            for w in writes:
                self.last_w[w] = tag
                self.readers[w] = []
        return tag

    def wait(self, eng, reads):
        self.op(eng, None, reads=tuple(reads))

    def mark_pe(self):
        assert self.ops["tensor"], "mark_pe with no PE ops"
        rec = self.ops["tensor"][-1]
        assert rec["sem"] is None, "double mark"
        self.sem("pe")
        self.cnt["pe"] += 1
        rec["sem"], rec["amt"] = "pe", 1
        tag = ("pe", self.cnt["pe"])
        for kind, res, i in self.pe_pend:
            if kind == 'w':
                lw = self.last_w.get(res)
                if isinstance(lw, tuple) and lw[0] == 'PEPEND':
                    self.last_w[res] = tag
            else:
                lst = self.readers.get(res, [])
                if i < len(lst) and isinstance(lst[i], tuple) and lst[i][0] == 'PEPEND':
                    lst[i] = tag
        self.pe_pend = []


def build_program():
    nc = bass.Bass("TRN2", target_bir_lowering=False, num_devices=8,
                   detect_race_conditions=False)

    def din(name, shape, dtype=F32):
        return nc.dram_tensor(name, list(shape), dtype, kind="ExternalInput").ap()

    XT = din("xt", (D, T))
    WRT = din("wrt", (128, DCH * 64))
    AAL = din("aal", (8, 2, 128, 2048))
    BAL = din("bal", (16, 128, 2048))
    WQKV = din("wqkv", (3, NH, 2, 128, 2048))
    WO = din("wo", (DCH, 2, 128, 2048))
    IDN32 = din("idn32", (128, 128))
    ONESROW = din("onesrow", (1, 128))
    ONESRB = din("onesrb", (1, 128), BF16)
    ONESB = din("onesb", (128, 2), BF16)
    MASKB = din("maskb", (128, 896), BF16)
    REP = din("rep", (64, KR), BF16)
    NONCE = din("nonce", (1, 8), I32)

    OUT = nc.dram_tensor("out", [D, T], F32, kind="ExternalOutput").ap()

    WQKVBF = nc.dram_tensor("wqkvbf", [3, NH, 128, DCH * 128], BF16,
                            kind="Internal").ap()
    XSP = nc.dram_tensor("xsp", [DCH, 128, T], BF16, kind="Internal").ap()
    Y2L = nc.dram_tensor("y2l", [DCH, 128, T], BF16, kind="Internal").ap()
    OL = nc.dram_tensor("ol", [NH, 128, S], BF16, kind="Internal").ap()
    SHY = nc.dram_tensor("shy", [2, DCH, 128, T], BF16, kind="Internal",
                         addr_space="Shared").ap()
    OSH = nc.dram_tensor("osh", [2, NH, 128, S], BF16, kind="Internal",
                         addr_space="Shared").ap()
    SHF = nc.dram_tensor("shf", [4, 2, 1, 2], I32, kind="Internal",
                         addr_space="Shared").ap()

    A16 = Arena(92736)
    A16.buf("maskb", 0, 896)
    A16.buf("onesb", 896, 2)
    A16.buf("onesrb", 928, 128)
    A16.buf("xtb", 1056, DCH * T)
    A16.buf("yh", 33824, DCH * 512)
    A16.buf("wrtb", 50208, 2048)
    A16.buf("repb", 52256, 1024)
    A16.buf("gexp", 53280, 4096)
    A16.buf("y2stg", 53280, 2048)
    A16.buf("hg", 57376, 4096)
    A16.buf("xsq", 89632, 1024)
    A16.buf("row16", 62496, 1024)
    A16.buf("gts", 63520, 512)
    A16.buf("pnl16", 81440, 8192)
    A16.buf("ytmp", 72224, 512)
    A16.buf("rowb", 72736, 2048)
    A16.buf("y2f", 1056, DCH * S)
    A16.buf("qbuf", 66592, 4096)
    A16.buf("kbuf", 70688, 4096)
    A16.buf("vtok", 74784, 4096)
    A16.buf("vtmp", 78880, 512)
    A16.buf("usq", 78880, 512)
    A16.buf("ebuf", 79392, 2048)
    A16.buf("qkvp", 81440, 8192)
    A16.buf("ostg", 89632, 1024)
    A16.buf("pc16", 90656, 2048)
    A16.buf("ores", 33824, DCH * T)

    A32 = Arena(6528)
    A32.buf("idn", 0, 128)
    A32.buf("onesrow", 128, 128)
    A32.buf("xstage", 256, 2048)
    A32.buf("pnl32b", 256, 2048)
    A32.buf("pcst", 2304, 2048)
    A32.buf("pnl32", 2304, 2048)
    A32.buf("rows", 4352, 1024)
    A32.buf("gat", 5376, 512)
    A32.buf("rz", 5888, 640)
    A32.buf("ostage", 5376, 1024)

    prog = Pr()
    for s in ("pe", "dve", "act", "pool"):
        prog.sem(s)

    def Rk(region, idx=None):
        if idx is None:
            return (region,)
        return ("%s.%s" % (region, idx), region)

    def Wk(arena, region, idx=None):
        return tuple(list(Rk(region, idx)) + arena.overlaps(region))

    W16 = lambda region, idx=None: Wk(A16, region, idx)
    W32 = lambda region, idx=None: Wk(A32, region, idx)

    import contextlib
    est = contextlib.ExitStack()
    AR16_t = est.enter_context(nc.sbuf_tensor("AR16", [128, A16.total], BF16))
    AR32_t = est.enter_context(nc.sbuf_tensor("AR32", [128, A32.total], F32))
    SNON_t = est.enter_context(nc.sbuf_tensor("SNON", [1, 8], I32))
    SFLG_t = est.enter_context(nc.sbuf_tensor("SFLG", [1, 4], I32))
    pM_t = est.enter_context(nc.psum_tensor("pM", [128, 1024], F32))
    pS_t = est.enter_context(nc.psum_tensor("pS", [128, 2048], F32))
    pZ_t = est.enter_context(nc.psum_tensor("pZ", [1, 512], F32))
    pU_t = est.enter_context(nc.psum_tensor("pU", [1, 512], F32))
    flpoll = est.enter_context(nc.semaphore("flpoll"))

    AR16, AR32 = AR16_t[:], AR32_t[:]
    SNON, SFLG = SNON_t[:], SFLG_t[:]
    pM, pS, pZ, pU = pM_t[:], pS_t[:], pZ_t[:], pU_t[:]

    def a16(name, p0=0, p1=128, off=0, cols=None):
        o, c = A16.bufs[name]
        if cols is None:
            cols = c - off
        return AR16[p0:p1, o + off:o + off + cols]

    def a32(name, p0=0, p1=128, off=0, cols=None):
        o, c = A32.bufs[name]
        if cols is None:
            cols = c - off
        return AR32[p0:p1, o + off:o + off + cols]

    dmaq = {}

    def dma(q, semname, fn, reads=(), writes=()):
        assert dmaq.setdefault(semname, q) == q, semname
        prog.op(q, fn, reads=reads, writes=writes, sem=prog.sem(semname), amt=16)

    def dma_pair(q, semname, fn_even, fn_odd, reads=(), writes=()):
        assert dmaq.setdefault(semname, q) == q, semname
        def f(eng, rt):
            return [fn_even(eng, rt), fn_odd(eng, rt)]
        prog.op(q, f, reads=reads, writes=writes, sem=prog.sem(semname), amt=32)

    def pe(fn, reads=(), writes=()):
        prog.op("tensor", fn, reads=reads, writes=writes)

    def dve(fn, reads=(), writes=()):
        prog.op("vector", fn, reads=reads, writes=writes)

    def act(fn, reads=(), writes=()):
        prog.op("scalar", fn, reads=reads, writes=writes)

    def pool(fn, reads=(), writes=()):
        prog.op("vector", fn, reads=reads, writes=writes)

    mark = prog.mark_pe
    RZ_NEGU, RZ_A, RZ_B, RZ_C = 512, 516, 520, 524

    dma("sync", "d_non", lambda e, rt: e.dma_start(out=SNON, in_=NONCE),
        writes=("snon",))
    for nm, dst, src in (
        ("maskb", a16("maskb"), MASKB),
        ("onesb", a16("onesb"), ONESB),
        ("onesrb", a16("onesrb", p1=1), ONESRB),
        ("idn", a32("idn"), IDN32),
        ("onesrow", a32("onesrow", p1=1, cols=128), ONESROW),
        ("repb", a16("repb", p1=64), REP),
    ):
        dma("sync", "d_cst",
            lambda e, rt, d=dst, s=src: e.dma_start(out=d, in_=s),
            writes=(nm,))

    dma("sync", "d_x0",
        lambda e, rt: e.dma_start(out=a32("xstage", cols=2048), in_=WRT),
        writes=W32("xstage", 0) + W32("xstage", 1))
    pool(lambda e, rt: e.tensor_copy(out=a16("wrtb"),
                                     in_=a32("xstage", cols=2048)),
         reads=Rk("xstage", 0) + Rk("xstage", 1), writes=W16("wrtb"))

    for ch in range(DCH):
        sl = ch % 2
        dma("sync", "d_x%d" % sl,
            lambda e, rt, ch=ch, sl=sl: e.dma_start(
                out=a32("xstage", off=1024 * sl, cols=1024),
                in_=XT[128 * ch:128 * (ch + 1), :]),
            writes=W32("xstage", sl))
        pool(lambda e, rt, ch=ch, sl=sl: e.tensor_copy(
                out=a16("xtb", off=T * ch, cols=T),
                in_=a32("xstage", off=1024 * sl, cols=1024)),
             reads=Rk("xstage", sl), writes=W16("xtb", ch))
        for hh in range(2):
            pool(lambda e, rt, ch=ch, hh=hh: e.tensor_mul(
                    out=a16("xsq", off=512 * hh, cols=512),
                    in0=a16("xtb", off=T * ch + 512 * hh, cols=512),
                    in1=a16("xtb", off=T * ch + 512 * hh, cols=512)),
                 reads=Rk("xtb", ch), writes=W16("xsq", hh))
            ps_sum = pZ if hh == 0 else pS[0:1, 0:512]
            ps_sq = pU if hh == 0 else pS[0:1, 512:1024]
            k_sum = "pZ" if hh == 0 else "pS.0"
            k_sq = "pU" if hh == 0 else "pS.1"
            pe(lambda e, rt, ch=ch, hh=hh, ps=ps_sum: e.matmul(
                    ps, a16("onesb", cols=1),
                    a16("xtb", off=T * ch + 512 * hh, cols=512),
                    start=(ch == 0), stop=(ch == DCH - 1)),
               reads=Rk("xtb", ch) + ("onesb",), writes=(k_sum,))
            pe(lambda e, rt, ch=ch, hh=hh, ps=ps_sq: e.matmul(
                    ps, a16("onesb", cols=1),
                    a16("xsq", off=512 * hh, cols=512),
                    start=(ch == 0), stop=(ch == DCH - 1)),
               reads=Rk("xsq", hh) + ("onesb",), writes=(k_sq,))
            mark()

    def emit_precast(h, p3):
        for hf in range(2):
            dma("sync", "d_pc",
                lambda e, rt, h=h, p3=p3, hf=hf: e.dma_start(
                    out=a32("pcst", cols=2048), in_=WQKV[p3, h, hf]),
                writes=W32("pcst"))
            pool(lambda e, rt: e.tensor_copy(out=a16("pc16", cols=2048),
                                             in_=a32("pcst", cols=2048)),
                 reads=Rk("pcst"), writes=W16("pc16"))
            dma("scalar", "d_pcw",
                lambda e, rt, h=h, p3=p3, hf=hf: e.dma_start(
                    out=WQKVBF[p3, h][:, 2048 * hf:2048 * (hf + 1)],
                    in_=a16("pc16", cols=2048)),
                reads=Rk("pc16"), writes=("wbf.%d.%d" % (p3, h), "wbf"))

    pc_iter = [(h, p3) for h in range(NH) for p3 in range(3)]
    pc_pos = [0]

    def precast_some(k):
        for _ in range(k):
            if pc_pos[0] < len(pc_iter):
                emit_precast(*pc_iter[pc_pos[0]])
                pc_pos[0] += 1

    def ln_finalize(ps_sum, k_sum, ps_sq, k_sq, slot):
        rbo = 1024 * slot
        act(lambda e, rt: e.activation(out=a32("rows", p1=1, off=0, cols=512),
                                       in_=ps_sum, func=AF.Identity,
                                       scale=INV_D),
            reads=(), writes=(k_sum, "r_rm") + W32("rows"))
        act(lambda e, rt: e.activation(out=a32("rows", p1=1, off=512, cols=512),
                                       in_=ps_sq, func=AF.Identity,
                                       scale=INV_D),
            reads=(), writes=(k_sq, "r_rstd"))
        dve(lambda e, rt: e.tensor_scalar(
                out=a32("rows", p1=1, off=512, cols=512),
                in0=a32("rows", p1=1, off=512, cols=512),
                scalar1=EPS, scalar2=None, op0=ALU.add),
            reads=("r_rstd",), writes=("r_rstd",))
        dve(lambda e, rt: e.tensor_mul(out=a32("gat", p1=1, cols=512),
                                       in0=a32("rows", p1=1, off=0, cols=512),
                                       in1=a32("rows", p1=1, off=0, cols=512)),
            reads=("r_rm",), writes=W32("gat") + ("r_msq",))
        dve(lambda e, rt: e.tensor_sub(out=a32("rows", p1=1, off=512, cols=512),
                                       in0=a32("rows", p1=1, off=512, cols=512),
                                       in1=a32("gat", p1=1, cols=512)),
            reads=("r_msq", "r_rstd"), writes=("r_rstd",))
        act(lambda e, rt: e.activation(out=a32("gat", p1=1, cols=512),
                                       in_=a32("rows", p1=1, off=512, cols=512),
                                       func=AF.Ln),
            reads=("r_rstd",), writes=("r_msq",))
        act(lambda e, rt: e.activation(out=a32("rows", p1=1, off=512, cols=512),
                                       in_=a32("gat", p1=1, cols=512),
                                       func=AF.Exp, scale=-0.5),
            reads=("r_msq",), writes=("r_rstd",))
        dve(lambda e, rt: e.tensor_mul(out=a32("rows", p1=1, off=0, cols=512),
                                       in0=a32("rows", p1=1, off=0, cols=512),
                                       in1=a32("rows", p1=1, off=512, cols=512)),
            reads=("r_rm", "r_rstd"), writes=("r_rm",))
        dve(lambda e, rt: e.tensor_copy(out=a16("row16", p1=1, off=0, cols=512),
                                        in_=a32("rows", p1=1, off=512, cols=512)),
            reads=("r_rstd",), writes=W16("row16", 0))
        dve(lambda e, rt: e.tensor_copy(out=a16("row16", p1=1, off=512, cols=512),
                                        in_=a32("rows", p1=1, off=0, cols=512)),
            reads=("r_rm",), writes=W16("row16", 1))
        pe(lambda e, rt: e.matmul(pM[:, 0:512], a16("onesrb", p1=1),
                                  a16("row16", p1=1, off=0, cols=512),
                                  start=True, stop=True),
           reads=Rk("row16", 0) + ("onesrb",), writes=("pM.0",))
        mark()
        pe(lambda e, rt: e.matmul(pM[:, 512:1024], a16("onesrb", p1=1),
                                  a16("row16", p1=1, off=512, cols=512),
                                  start=True, stop=True),
           reads=Rk("row16", 1) + ("onesrb",), writes=("pM.1",))
        mark()
        dve(lambda e, rt, rbo=rbo: e.tensor_copy(
                out=a16("rowb", off=rbo, cols=512), in_=pM[:, 0:512]),
            reads=(), writes=("pM.0",) + W16("rowb", 2 * slot))
        dve(lambda e, rt, rbo=rbo: e.tensor_copy(
                out=a16("rowb", off=rbo + 512, cols=512), in_=pM[:, 512:1024]),
            reads=(), writes=("pM.1",) + W16("rowb", 2 * slot + 1))

    def ln_apply(ch, hh, slot, dst_ap_fn, dst_w):
        rbo = 1024 * slot
        dve(lambda e, rt, ch=ch, hh=hh, rbo=rbo: e.tensor_mul(
                out=a16("ytmp"),
                in0=a16("xtb", off=T * ch + 512 * hh, cols=512),
                in1=a16("rowb", off=rbo, cols=512)),
            reads=Rk("xtb", ch) + Rk("rowb", 2 * slot), writes=W16("ytmp"))
        dve(lambda e, rt, rbo=rbo: e.tensor_sub(
                out=dst_ap_fn(), in0=a16("ytmp"),
                in1=a16("rowb", off=rbo + 512, cols=512)),
            reads=("ytmp",) + Rk("rowb", 2 * slot + 1), writes=dst_w)

    def emit_ffn(hh, final, k_sum=None, k_sq=None, ps_sum=None, ps_sq=None,
                 precast_budget=0):
        for ch in range(DCH):
            pe(lambda e, rt, ch=ch: e.matmul(
                    pS[0:64, 0:512],
                    a16("wrtb", off=64 * ch, cols=64),
                    a16("yh", off=512 * ch, cols=512),
                    start=(ch == 0), stop=(ch == DCH - 1)),
               reads=Rk("yh", ch) + ("wrtb",), writes=("pS.0",))
        mark()
        dve(lambda e, rt: e.tensor_copy(out=a32("gat", p1=64),
                                        in_=pS[0:64, 0:512]),
            reads=(), writes=("pS.0",) + W32("gat") + ("r_msq",))
        for t4 in range(4):
            pe(lambda e, rt, t4=t4: e.transpose(
                    pS[:, 512:576],
                    a32("gat", p0=0, p1=64, off=128 * t4, cols=128),
                    a32("idn", p1=64, cols=64)),
               reads=("gat", "r_msq", "idn"), writes=("pS.1",))
            mark()
            dve(lambda e, rt, t4=t4: e.tensor_copy(
                    out=a32("rows", off=64 * t4, cols=64),
                    in_=pS[:, 512:576]),
                reads=(), writes=("pS.1",) + W32("rows") + ("r_ss.%d" % t4,))
        for t4 in range(4):
            sS = lambda t4=t4: a32("rows", off=64 * t4, cols=64)
            wA = lambda: a32("rows", off=256, cols=64)
            wB = lambda: a32("rows", off=320, cols=64)
            geb = lambda: a32("rows", off=384, cols=64)
            e0b = lambda: a32("rows", off=448, cols=64)
            m0 = lambda: a32("rows", off=512, cols=1)
            mx = lambda: a32("rows", off=513, cols=1)
            t16 = lambda: a32("rows", off=514, cols=1)
            zz = lambda: a32("rows", off=515, cols=1)
            dve(lambda e, rt, sS=sS: e.reduce_max(out=m0(), in_=sS(),
                                                  axis=AX.X, negate=True),
                reads=("r_ss.%d" % t4,), writes=("r_m0",))
            dve(lambda e, rt, sS=sS: e.tensor_copy(out=wA(), in_=sS()),
                reads=("r_ss.%d" % t4,), writes=("r_wa",))
            cur, nxt, ck, nk_ = wA, wB, "r_wa", "r_wb"
            for it in range(15):
                dve(lambda e, rt, cur=cur: e.reduce_max(out=mx(), in_=cur(),
                                                        axis=AX.X),
                    reads=(ck,), writes=("r_mx",))
                dve(lambda e, rt, cur=cur: e.tensor_scalar(
                        out=geb(), in0=cur(), scalar1=mx(), scalar2=None,
                        op0=ALU.is_ge),
                    reads=(ck, "r_mx"), writes=("r_ge",))
                dve(lambda e, rt, cur=cur, nxt=nxt: e.scalar_tensor_tensor(
                        out=nxt(), in0=geb(), scalar=NEG_BIG, in1=cur(),
                        op0=ALU.mult, op1=ALU.add),
                    reads=("r_ge", ck), writes=(nk_,))
                cur, nxt, ck, nk_ = nxt, cur, nk_, ck
            dve(lambda e, rt, cur=cur: e.reduce_max(out=t16(), in_=cur(),
                                                    axis=AX.X),
                reads=(ck,), writes=("r_t16",))
            act(lambda e, rt, sS=sS: e.activation(out=e0b(), in_=sS(),
                                                  func=AF.Exp, bias=m0()),
                reads=("r_ss.%d" % t4, "r_m0"), writes=("r_e0",))
            dve(lambda e, rt, sS=sS: e.tensor_scalar(
                    out=geb(), in0=sS(), scalar1=t16(), scalar2=None,
                    op0=ALU.is_ge),
                reads=("r_ss.%d" % t4, "r_t16"), writes=("r_ge",))
            dve(lambda e, rt: e.tensor_mul(out=e0b(), in0=e0b(), in1=geb()),
                reads=("r_e0", "r_ge"), writes=("r_e0",))
            dve(lambda e, rt: e.reduce_sum(out=zz(), in_=e0b(), axis=AX.X),
                reads=("r_e0",), writes=("r_zz",))
            dve(lambda e, rt: e.reciprocal(out=mx(), in_=zz()),
                reads=("r_zz",), writes=("r_mx",))
            dve(lambda e, rt: e.tensor_scalar(out=e0b(), in0=e0b(),
                                              scalar1=mx(), scalar2=None,
                                              op0=ALU.mult),
                reads=("r_e0", "r_mx"), writes=("r_e0",))
            pe(lambda e, rt: e.transpose(pS[0:64, 640:768], e0b(), a32("idn")),
               reads=("r_e0", "idn"), writes=("pS.1",))
            mark()
            dve(lambda e, rt, t4=t4: e.tensor_copy(
                    out=a16("gts", p1=64, off=128 * t4, cols=128),
                    in_=pS[0:64, 640:768]),
                reads=(), writes=("pS.1",) + W16("gts", t4))
            if BUILD_UPTO == 161 and t4 == 0:
                raise _StopPlan()
        if BUILD_UPTO == 162:
            raise _StopPlan()
        for j in range(8):
            pe(lambda e, rt, j=j: e.matmul(
                    pS[:, 1024:1536],
                    a16("repb", p1=64, off=128 * j, cols=128),
                    a16("gts", p1=64), start=True, stop=True),
               reads=("repb", "gts.0", "gts.1", "gts.2", "gts.3", "gts"),
               writes=("pS.2",))
            mark()
            dve(lambda e, rt, j=j: e.tensor_copy(
                    out=a16("gexp", off=512 * j, cols=512),
                    in_=pS[:, 1024:1536]),
                reads=(), writes=("pS.2",) + W16("gexp", j))
        for j in range(8):
            for hf in range(2):
                psl = (2 * j + hf) % 2
                pn = "pnl32" if psl == 0 else "pnl32b"
                dma("sync", "d_pnl%d" % psl,
                    lambda e, rt, j=j, hf=hf, pn=pn: e.dma_start(
                        out=a32(pn, cols=2048), in_=AAL[j, hf]),
                    writes=W32(pn))
                pool(lambda e, rt, j=j, hf=hf, pn=pn: e.tensor_copy(
                        out=a16("pnl16", off=4096 * (j % 2) + 2048 * hf,
                                cols=2048),
                        in_=a32(pn, cols=2048)),
                     reads=Rk(pn), writes=W16("pnl16", j % 2))
            for ch in range(DCH):
                pe(lambda e, rt, j=j, ch=ch: e.matmul(
                        pM[:, 512 * (j % 2):512 * (j % 2) + 512],
                        a16("pnl16", off=4096 * (j % 2) + 128 * ch, cols=128),
                        a16("yh", off=512 * ch, cols=512),
                        start=(ch == 0), stop=(ch == DCH - 1)),
                   reads=Rk("pnl16", j % 2) + Rk("yh", ch),
                   writes=("pM.%d" % (j % 2),))
            mark()
            if BUILD_UPTO == 16:
                dve(lambda e, rt, j=j: e.tensor_copy(
                        out=a16("hg", off=512 * j, cols=512),
                        in_=pM[:, 512 * (j % 2):512 * (j % 2) + 512]),
                    reads=(),
                    writes=("pM.%d" % (j % 2),) + W16("hg", j) + ("hraw.%d" % j,))
            else:
                dve(lambda e, rt, j=j: e.tensor_mul(
                        out=a16("hg", off=512 * j, cols=512),
                        in0=pM[:, 512 * (j % 2):512 * (j % 2) + 512],
                        in1=a16("gexp", off=512 * j, cols=512)),
                    reads=Rk("gexp", j),
                    writes=("pM.%d" % (j % 2),) + W16("hg", j))
            precast_some(precast_budget)
        if BUILD_UPTO == 16:
            for j in range(8):
                dve(lambda e, rt, j=j: e.tensor_copy(
                        out=a32("ostage", off=0, cols=512),
                        in_=a16("gexp", off=512 * j, cols=512)),
                    reads=Rk("gexp", j), writes=W32("ostage", 0))
                dma("scalar", "d_out",
                    lambda e, rt, j=j: e.dma_start(
                        out=OUT[128 * j:128 * (j + 1), 0:512],
                        in_=a32("ostage", off=0, cols=512)),
                    reads=Rk("ostage", 0), writes=("hbm_out",))
                dve(lambda e, rt, j=j: e.tensor_copy(
                        out=a32("ostage", off=512, cols=512),
                        in_=a16("hg", off=512 * j, cols=512)),
                    reads=("hraw.%d" % j,) + Rk("hg", j), writes=W32("ostage", 1))
                dma("scalar", "d_out",
                    lambda e, rt, j=j: e.dma_start(
                        out=OUT[1280 + 128 * j:1280 + 128 * (j + 1), 512:1024],
                        in_=a32("ostage", off=512, cols=512)),
                    reads=Rk("ostage", 1), writes=("hbm_out",))
            dve(lambda e, rt: e.tensor_copy(out=a32("ostage", off=0, cols=256),
                                            in_=a32("rows", off=0, cols=256)),
                reads=("r_ss.0", "r_ss.1", "r_ss.2", "r_ss.3"),
                writes=W32("ostage", 0))
            dma("scalar", "d_out",
                lambda e, rt: e.dma_start(out=OUT[1152:1280, 0:256],
                                          in_=a32("ostage", off=0, cols=256)),
                reads=Rk("ostage", 0), writes=("hbm_out",))
            prog.wait("scalar", ["hbm_out"])
            raise _StopPlan()
        for ii in range(16):
            pn = "pnl32" if ii % 2 == 0 else "pnl32b"
            dma("sync", "d_pnl%d" % (ii % 2),
                lambda e, rt, ii=ii, pn=pn: e.dma_start(
                    out=a32(pn, cols=2048), in_=BAL[ii]),
                writes=W32(pn))
            pool(lambda e, rt, ii=ii, pn=pn: e.tensor_copy(
                    out=a16("pnl16", off=4096 * (ii % 2), cols=2048),
                    in_=a32(pn, cols=2048)),
                 reads=Rk(pn), writes=W16("pnl16", ii % 2))
            for s2 in range(2):
                i = 2 * ii + s2
                for r in range(8):
                    pe(lambda e, rt, ii=ii, s2=s2, r=r, i=i: e.matmul(
                            pM[:, 512 * (i % 2):512 * (i % 2) + 512],
                            a16("pnl16", off=4096 * (ii % 2) + 256 * r
                                + 128 * s2, cols=128),
                            a16("hg", off=512 * r, cols=512),
                            start=(r == 0), stop=(r == 7)),
                       reads=Rk("pnl16", ii % 2) + Rk("hg", r),
                       writes=("pM.%d" % (i % 2),))
                mark()
                if final:
                    osl = i % 2
                    dve(lambda e, rt, i=i, hh=hh, osl=osl: e.tensor_tensor(
                            out=a32("ostage", off=512 * osl, cols=512),
                            in0=a16("xtb", off=T * i + 512 * hh, cols=512),
                            in1=pM[:, 512 * (i % 2):512 * (i % 2) + 512],
                            op=ALU.add),
                        reads=Rk("xtb", i),
                        writes=("pM.%d" % (i % 2),) + W32("ostage", osl))
                    dma("scalar", "d_out",
                        lambda e, rt, i=i, hh=hh, osl=osl: e.dma_start(
                            out=OUT[128 * i:128 * (i + 1),
                                    512 * hh:512 * (hh + 1)],
                            in_=a32("ostage", off=512 * osl, cols=512)),
                        reads=Rk("ostage", osl), writes=("hbm_out",))
                else:
                    dve(lambda e, rt, i=i, hh=hh: e.tensor_tensor(
                            out=a16("xtb", off=T * i + 512 * hh, cols=512),
                            in0=a16("xtb", off=T * i + 512 * hh, cols=512),
                            in1=pM[:, 512 * (i % 2):512 * (i % 2) + 512],
                            op=ALU.add),
                        reads=(),
                        writes=("pM.%d" % (i % 2),) + W16("xtb", i))
                    pool(lambda e, rt, i=i, hh=hh, s2=s2: e.tensor_mul(
                            out=a16("xsq", off=512 * s2, cols=512),
                            in0=a16("xtb", off=T * i + 512 * hh, cols=512),
                            in1=a16("xtb", off=T * i + 512 * hh, cols=512)),
                         reads=Rk("xtb", i), writes=W16("xsq", s2))
                    pe(lambda e, rt, i=i, hh=hh, ps=ps_sum: e.matmul(
                            ps, a16("onesb", cols=1),
                            a16("xtb", off=T * i + 512 * hh, cols=512),
                            start=(i == 0), stop=(i == DCH - 1)),
                       reads=Rk("xtb", i) + ("onesb",), writes=(k_sum,))
                    pe(lambda e, rt, i=i, s2=s2, ps=ps_sq: e.matmul(
                            ps, a16("onesb", cols=1),
                            a16("xsq", off=512 * s2, cols=512),
                            start=(i == 0), stop=(i == DCH - 1)),
                       reads=Rk("xsq", s2) + ("onesb",), writes=(k_sq,))
                    mark()
            precast_some(precast_budget)

    def barrier(bidx, payload_reads):
        dma_pair("sync", "d_flg",
                 lambda e, rt, bidx=bidx: e.dma_start(
                     out=SHF[bidx, 0][0:1, 0:1],
                     in_=SNON[0:1, bidx:bidx + 1], cond=rt["iseven"]),
                 lambda e, rt, bidx=bidx: e.dma_start(
                     out=SHF[bidx, 1][0:1, 0:1],
                     in_=SNON[0:1, bidx:bidx + 1], cond=rt["isodd"]),
                 reads=tuple(payload_reads) + ("snon",),
                 writes=("flag%d" % bidx,))

        def pollfn(e, rt, bidx=bidx):
            nval = e.value_load(SNON[0:1, bidx:bidx + 1])
            fv = SHF[bidx].rearrange("a b c -> b (a c)")

            def cond():
                e.sem_clear(flpoll)
                e.dma_start(out=SFLG, in_=fv).then_inc(flpoll, 16)
                e.wait_ge(flpoll, 16)
                va = e.value_load(SFLG[0:1, 0:1])
                vb = e.value_load(SFLG[0:1, 2:3])
                return ((va == nval) & (vb == nval)) == 0

            with e.While(cond):
                e.nop()
            return e.nop()

        prog.op("sync", pollfn, reads=("flag%d" % bidx, "snon"),
                writes=("xbar%d" % bidx,), sem=prog.sem("bar%d" % bidx), amt=1)

    def emit_dump():
        for ch in range(DCH):
            for tt in range(2):
                sl = tt
                dve(lambda e, rt, ch=ch, tt=tt, sl=sl: e.tensor_copy(
                        out=a32("ostage", off=512 * sl, cols=512),
                        in_=a16("xtb", off=T * ch + 512 * tt, cols=512)),
                    reads=Rk("xtb", ch), writes=W32("ostage", sl))
                dma("scalar", "d_out",
                    lambda e, rt, ch=ch, tt=tt, sl=sl: e.dma_start(
                        out=OUT[128 * ch:128 * (ch + 1),
                                512 * tt:512 * (tt + 1)],
                        in_=a32("ostage", off=512 * sl, cols=512)),
                    reads=Rk("ostage", sl), writes=("hbm_out",))
        prog.wait("scalar", ["hbm_out"])

    def emit_dump_yh():
        for ch in range(DCH):
            dve(lambda e, rt, ch=ch: e.tensor_copy(
                    out=a32("ostage", off=0, cols=512),
                    in_=a16("yh", off=512 * ch, cols=512)),
                reads=Rk("yh", ch), writes=W32("ostage", 0))
            dma("scalar", "d_out",
                lambda e, rt, ch=ch: e.dma_start(
                    out=OUT[128 * ch:128 * (ch + 1), 0:512],
                    in_=a32("ostage", off=0, cols=512)),
                reads=Rk("ostage", 0), writes=("hbm_out",))
        prog.wait("scalar", ["hbm_out"])

    def emit_attention():
        for h in range(NH):
            hp = h % 2
            for p3 in range(3):
                psl = (3 * h + p3) % 2
                dma("sync", "d_qp%d" % psl,
                    lambda e, rt, p3=p3, h=h, psl=psl: e.dma_start(
                        out=a16("qkvp", off=4096 * psl, cols=4096),
                        in_=WQKVBF[p3, h]),
                    reads=("wbf.%d.%d" % (p3, h),), writes=W16("qkvp", psl))
                for tt in range(4):
                    for ch in range(DCH):
                        pe(lambda e, rt, tt=tt, ch=ch, psl=psl: e.matmul(
                                pM[:, 512 * (tt % 2):512 * (tt % 2) + 512],
                                a16("qkvp", off=4096 * psl + 128 * ch, cols=128),
                                a16("y2f", off=2048 * ch + 512 * tt, cols=512),
                                start=(ch == 0), stop=(ch == DCH - 1)),
                           reads=Rk("qkvp", psl) + ("y2f.0", "y2f.1", "y2f"),
                           writes=("pM.%d" % (tt % 2),))
                    mark()
                    if p3 == 0:
                        act(lambda e, rt, tt=tt, hp=hp: e.activation(
                                out=a16("qbuf", off=2048 * hp + 512 * tt,
                                        cols=512),
                                in_=pM[:, 512 * (tt % 2):512 * (tt % 2) + 512],
                                func=AF.Identity, scale=SCALE_QK),
                            reads=(),
                            writes=("pM.%d" % (tt % 2),) + W16("qbuf", hp))
                    elif p3 == 1:
                        act(lambda e, rt, tt=tt, hp=hp: e.copy(
                                out=a16("kbuf", off=2048 * hp + 512 * tt,
                                        cols=512),
                                in_=pM[:, 512 * (tt % 2):512 * (tt % 2) + 512]),
                            reads=(),
                            writes=("pM.%d" % (tt % 2),) + W16("kbuf", hp))
                    else:
                        act(lambda e, rt, tt=tt: e.copy(
                                out=a16("vtmp"),
                                in_=pM[:, 512 * (tt % 2):512 * (tt % 2) + 512]),
                            reads=(),
                            writes=("pM.%d" % (tt % 2),) + W16("vtmp"))
                        for s in range(4):
                            dma("sync", "d_vt",
                                lambda e, rt, tt=tt, s=s, hp=hp:
                                e.dma_start_transpose(
                                    out=a16("vtok", off=2048 * hp
                                            + 128 * (4 * tt + s), cols=128),
                                    in_=a16("vtmp", off=128 * s, cols=128)),
                                reads=("vtmp",), writes=W16("vtok", hp))
            for bname, dst_off in (("qbuf", RZ_A), ("kbuf", RZ_B)):
                for piece in range(4):
                    pool(lambda e, rt, bname=bname, piece=piece, hp=hp:
                         e.tensor_mul(
                            out=a16("usq"),
                            in0=a16(bname, off=2048 * hp + 512 * piece, cols=512),
                            in1=a16(bname, off=2048 * hp + 512 * piece, cols=512)),
                         reads=Rk(bname, hp), writes=W16("usq"))
                    pe(lambda e, rt: e.matmul(pU, a16("onesb", cols=1),
                                              a16("usq"), start=True, stop=True),
                       reads=("usq", "onesb"), writes=("pU",))
                    mark()
                    dve(lambda e, rt: e.tensor_copy(
                            out=a32("rows", p1=1, off=0, cols=512), in_=pU),
                        reads=(), writes=("pU", "r_urow") + W32("rows"))
                    dve(lambda e, rt: e.reduce_max(
                            out=a32("rz", p1=1, off=RZ_C, cols=1),
                            in_=a32("rows", p1=1, off=0, cols=512), axis=AX.X),
                        reads=("r_urow",), writes=("r_uc",))
                    if piece == 0:
                        dve(lambda e, rt, dst_off=dst_off: e.tensor_copy(
                                out=a32("rz", p1=1, off=dst_off, cols=1),
                                in_=a32("rz", p1=1, off=RZ_C, cols=1)),
                            reads=("r_uc",), writes=("r_n2.%d" % dst_off,))
                    else:
                        dve(lambda e, rt, dst_off=dst_off: e.tensor_tensor(
                                out=a32("rz", p1=1, off=dst_off, cols=1),
                                in0=a32("rz", p1=1, off=dst_off, cols=1),
                                in1=a32("rz", p1=1, off=RZ_C, cols=1),
                                op=ALU.max),
                            reads=("r_uc", "r_n2.%d" % dst_off),
                            writes=("r_n2.%d" % dst_off,))
            dve(lambda e, rt: e.tensor_mul(
                    out=a32("rz", p1=1, off=RZ_C, cols=1),
                    in0=a32("rz", p1=1, off=RZ_A, cols=1),
                    in1=a32("rz", p1=1, off=RZ_B, cols=1)),
                reads=("r_n2.%d" % RZ_A, "r_n2.%d" % RZ_B), writes=("r_uc",))
            act(lambda e, rt: e.activation(
                    out=a32("rz", p1=1, off=RZ_A, cols=1),
                    in_=a32("rz", p1=1, off=RZ_C, cols=1), func=AF.Ln),
                reads=("r_uc",), writes=("r_n2.%d" % RZ_A,))
            act(lambda e, rt: e.activation(
                    out=a32("rz", p1=1, off=RZ_C, cols=1),
                    in_=a32("rz", p1=1, off=RZ_A, cols=1),
                    func=AF.Exp, scale=0.5),
                reads=("r_n2.%d" % RZ_A,), writes=("r_uc",))
            dve(lambda e, rt: e.tensor_scalar(
                    out=a32("rz", p1=1, off=RZ_A, cols=1),
                    in0=a32("rz", p1=1, off=RZ_C, cols=1), scalar1=-1.0,
                    scalar2=None, op0=ALU.mult),
                reads=("r_uc",), writes=("r_n2.%d" % RZ_A,))
            pe(lambda e, rt: e.matmul(pM[:, 0:1],
                                      a32("onesrow", p1=1, cols=128),
                                      a32("rz", p1=1, off=RZ_A, cols=1),
                                      start=True, stop=True),
               reads=("r_n2.%d" % RZ_A, "onesrow"), writes=("pM.0",))
            mark()
            dve(lambda e, rt: e.tensor_copy(
                    out=a32("rz", off=RZ_NEGU, cols=1), in_=pM[:, 0:1]),
                reads=(), writes=("pM.0", "r_negub"))

            for J in range(4):
                ngr = 2 * J + 2
                nk = 4 * J + 4

                def s_group(g, J=J, hp=hp):
                    bk = 2 * (g % 2)
                    for s2 in range(2):
                        i = 2 * g + s2
                        pe(lambda e, rt, i=i, s2=s2, bk=bk, J=J, hp=hp:
                           e.matmul(
                               pS[:, 512 * (bk + s2):512 * (bk + s2) + 512],
                               a16("kbuf", off=2048 * hp + 128 * i, cols=128),
                               a16("qbuf", off=2048 * hp + 512 * J, cols=512),
                               start=True, stop=True),
                           reads=Rk("kbuf", hp) + Rk("qbuf", hp),
                           writes=("pS.%d" % (bk + s2),))
                    mark()

                def exp_group(g, J=J):
                    bk = 2 * (g % 2)
                    esl = g % 2
                    act(lambda e, rt, bk=bk, esl=esl: e.activation(
                            out=a16("ebuf", off=1024 * esl, cols=1024),
                            in_=pS[:, 512 * bk:512 * bk + 1024],
                            func=AF.Exp,
                            bias=a32("rz", off=RZ_NEGU, cols=1)),
                        reads=("r_negub",),
                        writes=("pS.%d" % bk, "pS.%d" % (bk + 1))
                        + W16("ebuf", esl))
                    if g >= 2 * J:
                        for s2 in range(2):
                            a = 2 * g + s2 - 4 * J
                            dve(lambda e, rt, s2=s2, a=a, esl=esl:
                                e.tensor_mul(
                                    out=a16("ebuf", off=1024 * esl + 512 * s2,
                                            cols=512),
                                    in0=a16("ebuf", off=1024 * esl + 512 * s2,
                                            cols=512),
                                    in1=a16("maskb", off=384 - 128 * a,
                                            cols=512)),
                                reads=("maskb", "ebuf.%d" % esl),
                                writes=("ebuf.%d" % esl,))

                def zpv_group(g, J=J, nk=nk, hp=hp):
                    esl = g % 2
                    for s2 in range(2):
                        i = 2 * g + s2
                        pe(lambda e, rt, i=i, s2=s2, esl=esl, nk=nk: e.matmul(
                                pZ, a16("onesb", cols=1),
                                a16("ebuf", off=1024 * esl + 512 * s2,
                                    cols=512),
                                start=(i == 0), stop=(i == nk - 1)),
                           reads=("ebuf.%d" % esl, "onesb"), writes=("pZ",))
                        pe(lambda e, rt, i=i, s2=s2, esl=esl, nk=nk, hp=hp:
                           e.matmul(
                               pM[:, 0:512],
                               a16("vtok", off=2048 * hp + 128 * i, cols=128),
                               a16("ebuf", off=1024 * esl + 512 * s2, cols=512),
                               start=(i == 0), stop=(i == nk - 1)),
                           reads=("ebuf.%d" % esl,) + Rk("vtok", hp),
                           writes=("pM.0",))
                    mark()

                s_group(0)
                if ngr > 1:
                    exp_group(0)
                    s_group(1)
                    for g in range(2, ngr):
                        zpv_group(g - 2)
                        exp_group(g - 1)
                        s_group(g)
                    zpv_group(ngr - 2)
                    exp_group(ngr - 1)
                    zpv_group(ngr - 1)
                else:
                    exp_group(0)
                    zpv_group(0)

                dve(lambda e, rt: e.tensor_copy(
                        out=a32("rows", p1=1, off=512, cols=512), in_=pZ),
                    reads=(), writes=("pZ", "r_zrow") + W32("rows"))
                dve(lambda e, rt: e.reciprocal(
                        out=a32("rows", p1=1, off=0, cols=512),
                        in_=a32("rows", p1=1, off=512, cols=512)),
                    reads=("r_zrow",), writes=("r_zrec",))
                pe(lambda e, rt: e.matmul(pM[:, 512:1024],
                                          a32("onesrow", p1=1, cols=128),
                                          a32("rows", p1=1, off=0, cols=512),
                                          start=True, stop=True),
                   reads=("r_zrec", "onesrow"), writes=("pM.1",))
                mark()
                dve(lambda e, rt: e.tensor_copy(out=a32("rz", cols=512),
                                                in_=pM[:, 512:1024]),
                    reads=(), writes=("pM.1", "r_rzs"))
                osl = J % 2
                dve(lambda e, rt, osl=osl: e.tensor_mul(
                        out=a16("ostg", off=512 * osl, cols=512),
                        in0=a32("rz", cols=512),
                        in1=pM[:, 0:512]),
                    reads=("r_rzs",),
                    writes=("pM.0",) + W16("ostg", osl))
                dma("scalar", "d_ol",
                    lambda e, rt, h=h, J=J, osl=osl: e.dma_start(
                        out=OL[h][:, 512 * J:512 * (J + 1)],
                        in_=a16("ostg", off=512 * osl, cols=512)),
                    reads=Rk("ostg", osl), writes=("ol",))
            precast_some(3)
            if BUILD_UPTO == 41 and h == 0:
                for nm, row0 in (("qbuf", 0), ("kbuf", 128), ("vtok", 256)):
                    for piece in range(4):
                        dve(lambda e, rt, nm=nm, piece=piece: e.tensor_copy(
                                out=a32("ostage", off=0, cols=512),
                                in_=a16(nm, off=512 * piece, cols=512)),
                            reads=(nm, nm + ".0"), writes=W32("ostage", 0))
                        dma("scalar", "d_out",
                            lambda e, rt, nm=nm, row0=row0, piece=piece:
                            e.dma_start(
                                out=OUT[row0:row0 + 128,
                                        512 * (piece % 2):512 * (piece % 2) + 512]
                                if False else
                                OUT[row0 + (piece // 2) * 512:row0 + (piece // 2) * 512 + 128,
                                    512 * (piece % 2):512 * (piece % 2) + 512],
                                in_=a32("ostage", off=0, cols=512)),
                            reads=Rk("ostage", 0), writes=("hbm_out",))
                for piece in range(4):
                    dma("sync", "d_dbg",
                        lambda e, rt, piece=piece: e.dma_start(
                            out=a16("ebuf", off=0, cols=512),
                            in_=OL[0][:, 512 * piece:512 * (piece + 1)]),
                        reads=("ol",), writes=W16("ebuf", 0))
                    dve(lambda e, rt: e.tensor_copy(
                            out=a32("ostage", off=512, cols=512),
                            in_=a16("ebuf", off=0, cols=512)),
                        reads=("ebuf.0",), writes=W32("ostage", 1))
                    dma("scalar", "d_out",
                        lambda e, rt, piece=piece: e.dma_start(
                            out=OUT[384 + (piece // 2) * 512:384 + (piece // 2) * 512 + 128,
                                    512 * (piece % 2):512 * (piece % 2) + 512],
                            in_=a32("ostage", off=512, cols=512)),
                        reads=Rk("ostage", 1), writes=("hbm_out",))
                for piece in range(4):
                    dve(lambda e, rt, piece=piece: e.tensor_copy(
                            out=a32("ostage", off=512, cols=512),
                            in_=a16("y2f", off=512 * piece, cols=512)),
                        reads=("y2f", "y2f.0", "y2f.1"), writes=W32("ostage", 1))
                    dma("scalar", "d_out",
                        lambda e, rt, piece=piece: e.dma_start(
                            out=OUT[1100 + (piece // 2) * 512:1100 + (piece // 2) * 512 + 128,
                                    512 * (piece % 2):512 * (piece % 2) + 512],
                            in_=a32("ostage", off=512, cols=512)),
                        reads=Rk("ostage", 1), writes=("hbm_out",))
                prog.wait("scalar", ["hbm_out"])
                raise _StopPlan()

    class _StopPlan(Exception):
        pass

    try:
        if BUILD_UPTO < 2:
            emit_dump()
            raise _StopPlan()
        ln_finalize(pZ, "pZ", pU, "pU", slot=0)
        ln_finalize(pS[0:1, 0:512], "pS.0", pS[0:1, 512:1024], "pS.1", slot=1)
        for hh in range(2):
            for ch in range(DCH):
                ln_apply(ch, hh, hh,
                         lambda ch=ch: a16("yh", off=512 * ch, cols=512),
                         W16("yh", ch))
            if BUILD_UPTO == 15:
                emit_dump_yh()
                raise _StopPlan()
            emit_ffn(hh, final=False, k_sum="pZ", k_sq="pU", ps_sum=pZ, ps_sq=pU,
                     precast_budget=2)
            if BUILD_UPTO == 43:
                dve(lambda e, rt: e.tensor_copy(
                        out=a32("ostage", p1=1, off=0, cols=512), in_=pZ),
                    reads=(), writes=("pZ",) + W32("ostage", 0))
                dve(lambda e, rt: e.tensor_copy(
                        out=a32("ostage", p1=1, off=512, cols=512), in_=pU),
                    reads=(), writes=("pU",) + W32("ostage", 1))
                dma("scalar", "d_out",
                    lambda e, rt: e.dma_start(out=OUT[0:1, 0:512],
                                              in_=a32("ostage", p1=1, off=0, cols=512)),
                    reads=Rk("ostage", 0), writes=("hbm_out",))
                dma("scalar", "d_out",
                    lambda e, rt: e.dma_start(out=OUT[1:2, 0:512],
                                              in_=a32("ostage", p1=1, off=512, cols=512)),
                    reads=Rk("ostage", 1), writes=("hbm_out",))
                prog.wait("scalar", ["hbm_out"])
                raise _StopPlan()
            if BUILD_UPTO == 17:
                emit_dump()
                raise _StopPlan()
            ln_finalize(pZ, "pZ", pU, "pU", slot=0)
            for ch in range(DCH):
                sl = ch % 2
                ln_apply(ch, hh, 0,
                         lambda ch=ch, sl=sl: a16("y2stg", off=1024 * sl, cols=512),
                         W16("y2stg", sl))
                dma("scalar", "d_y2l",
                    lambda e, rt, ch=ch, hh=hh, sl=sl: e.dma_start(
                        out=Y2L[ch][:, 512 * hh:512 * (hh + 1)],
                        in_=a16("y2stg", off=1024 * sl, cols=512)),
                    reads=Rk("y2stg", sl), writes=("y2l",))

        if BUILD_UPTO < 3:
            emit_dump()
        else:
            dma("scalar", "d_xsp",
                lambda e, rt: e.dma_start(
                    out=XSP.rearrange("c p t -> p c t"),
                    in_=a16("xtb").rearrange("p (c t) -> p c t", c=DCH)),
                reads=tuple("xtb.%d" % c for c in range(DCH)) + ("xtb",),
                writes=("hbm_xsp",))
            dma_pair("scalar", "d_y2c",
                     lambda e, rt: e.dma_start(out=SHY[0], in_=Y2L,
                                               cond=rt["iseven"]),
                     lambda e, rt: e.dma_start(out=SHY[1], in_=Y2L,
                                               cond=rt["isodd"]),
                     reads=("y2l",), writes=("shy",))
            barrier(0, ["shy"])
            for sl in range(2):
                dma("sync", "d_y2r",
                    lambda e, rt, sl=sl: e.dma_start(
                        out=a16("y2f").rearrange("p (c t) -> p c t", c=DCH)[
                            :, :, 1024 * sl:1024 * (sl + 1)],
                        in_=SHY[sl].rearrange("c p t -> p c t")),
                    reads=("xbar0",), writes=W16("y2f", sl))

            emit_attention()
            dma_pair("scalar", "d_oc",
                     lambda e, rt: e.dma_start(out=OSH[0], in_=OL,
                                               cond=rt["iseven"]),
                     lambda e, rt: e.dma_start(out=OSH[1], in_=OL,
                                               cond=rt["isodd"]),
                     reads=("ol",), writes=("osh",))
            barrier(1, ["osh"])

            dma("sync", "d_xtl",
                lambda e, rt: e.dma_start(
                    out=a16("xtb").rearrange("p (c t) -> p c t", c=DCH),
                    in_=XSP.rearrange("c p t -> p c t")),
                reads=("hbm_xsp", "xbar1"),
                writes=W16("xtb") + tuple("xtb.%d" % c for c in range(DCH)))
            for sl in range(2):
                dma_pair("scalar", "d_ore",
                         lambda e, rt, sl=sl: e.dma_start(
                             out=a16("ores").rearrange("p (c t) -> p c t", c=DCH)[
                                 :, 16 * sl:16 * (sl + 1), :],
                             in_=OSH[sl][:, :, 0:T].rearrange("c p t -> p c t"),
                             cond=rt["iseven"]),
                         lambda e, rt, sl=sl: e.dma_start(
                             out=a16("ores").rearrange("p (c t) -> p c t", c=DCH)[
                                 :, 16 * sl:16 * (sl + 1), :],
                             in_=OSH[sl][:, :, T:S].rearrange("c p t -> p c t"),
                             cond=rt["isodd"]),
                         reads=("xbar1",), writes=W16("ores", sl))

            if BUILD_UPTO == 45:

                for vd in range(DCH):

                    for tt in range(2):

                        dve(lambda e, rt, vd=vd, tt=tt: e.tensor_copy(

                                out=a32("ostage", off=512 * tt, cols=512),

                                in_=a16("ores", off=T * vd + 512 * tt, cols=512)),

                            reads=("ores", "ores.0", "ores.1"), writes=W32("ostage", tt))

                        dma("scalar", "d_out",

                            lambda e, rt, vd=vd, tt=tt: e.dma_start(

                                out=OUT[128 * vd:128 * (vd + 1), 512 * tt:512 * (tt + 1)],

                                in_=a32("ostage", off=512 * tt, cols=512)),

                            reads=Rk("ostage", tt), writes=("hbm_out",))

                prog.wait("scalar", ["hbm_out"])

                raise _StopPlan()

            for ii in range(DCH):
                for hf in range(2):
                    psl = (2 * ii + hf) % 2
                    pn = "pnl32" if psl == 0 else "pnl32b"
                    dma("sync", "d_pnl%d" % psl,
                        lambda e, rt, ii=ii, hf=hf, pn=pn: e.dma_start(
                            out=a32(pn, cols=2048), in_=WO[ii, hf]),
                        writes=W32(pn))
                    pool(lambda e, rt, ii=ii, hf=hf, pn=pn: e.tensor_copy(
                            out=a16("pnl16", off=4096 * (ii % 2) + 2048 * hf,
                                    cols=2048),
                            in_=a32(pn, cols=2048)),
                         reads=Rk(pn), writes=W16("pnl16", ii % 2))
                for tt in range(2):
                    for vd in range(DCH):
                        pe(lambda e, rt, ii=ii, tt=tt, vd=vd: e.matmul(
                                pM[:, 512 * tt:512 * (tt + 1)],
                                a16("pnl16", off=4096 * (ii % 2) + 128 * vd,
                                    cols=128),
                                a16("ores", off=T * vd + 512 * tt, cols=512),
                                start=(vd == 0), stop=(vd == DCH - 1)),
                           reads=Rk("pnl16", ii % 2) + Rk("ores", vd // 16),
                           writes=("pM.%d" % tt,))
                    mark()
                    dve(lambda e, rt, ii=ii, tt=tt: e.tensor_tensor(
                            out=a16("xtb", off=T * ii + 512 * tt, cols=512),
                            in0=a16("xtb", off=T * ii + 512 * tt, cols=512),
                            in1=pM[:, 512 * tt:512 * (tt + 1)],
                            op=ALU.add),
                        reads=(),
                        writes=("pM.%d" % tt,) + W16("xtb", ii))
                    if BUILD_UPTO >= 7:
                        pool(lambda e, rt, ii=ii, tt=tt: e.tensor_mul(
                                out=a16("xsq", off=512 * tt, cols=512),
                                in0=a16("xtb", off=T * ii + 512 * tt, cols=512),
                                in1=a16("xtb", off=T * ii + 512 * tt, cols=512)),
                             reads=Rk("xtb", ii), writes=W16("xsq", tt))
                        pe(lambda e, rt, ii=ii, tt=tt: e.matmul(
                                pS[0:1, 1024 * tt:1024 * tt + 512],
                                a16("onesb", cols=1),
                                a16("xtb", off=T * ii + 512 * tt, cols=512),
                                start=(ii == 0), stop=(ii == DCH - 1)),
                           reads=Rk("xtb", ii) + ("onesb",),
                           writes=("pS.%d" % (2 * tt),))
                        pe(lambda e, rt, ii=ii, tt=tt: e.matmul(
                                pS[0:1, 1024 * tt + 512:1024 * tt + 1024],
                                a16("onesb", cols=1),
                                a16("xsq", off=512 * tt, cols=512),
                                start=(ii == 0), stop=(ii == DCH - 1)),
                           reads=Rk("xsq", tt) + ("onesb",),
                           writes=("pS.%d" % (2 * tt + 1),))
                        mark()

            if BUILD_UPTO < 7:
                emit_dump()
            else:
                dma("sync", "d_x0",
                    lambda e, rt: e.dma_start(
                        out=a32("xstage", cols=2048).rearrange(
                            "p (c k) -> p c k", c=DCH),
                        in_=WRT.rearrange("(c p) k -> p c k", p=128)),
                    writes=W32("xstage", 0) + W32("xstage", 1))
                pool(lambda e, rt: e.tensor_copy(out=a16("wrtb"),
                                                 in_=a32("xstage", cols=2048)),
                     reads=Rk("xstage", 0) + Rk("xstage", 1), writes=W16("wrtb"))
                dma("sync", "d_cst",
                    lambda e, rt: e.dma_start(out=a16("repb", p1=64), in_=REP),
                    writes=W16("repb"))
                ln_finalize(pS[0:1, 0:512], "pS.0", pS[0:1, 512:1024], "pS.1",
                            slot=0)
                ln_finalize(pS[0:1, 1024:1536], "pS.2", pS[0:1, 1536:2048], "pS.3",
                            slot=1)
                for hh in range(2):
                    for ch in range(DCH):
                        ln_apply(ch, hh, hh,
                                 lambda ch=ch: a16("yh", off=512 * ch, cols=512),
                                 W16("yh", ch))
                    emit_ffn(hh, final=True)
                prog.wait("scalar", ["hbm_out"])
                barrier(2, ["ores", "xtb"])
    except _StopPlan:
        pass


    for semname, q in dmaq.items():
        if prog.cnt.get(semname, 0) > 0:
            prog.ops[q].append(dict(waits={semname: prog.cnt[semname]},
                                    fn=None, sem=None, amt=0))

    semmap = {}
    for sname in prog.semnames:
        semmap[sname] = est.enter_context(nc.semaphore("s_" + sname))

    def replay(engname, eng):
        rt = {}
        pid = eng.partition_id()
        par = pid % 2
        rt["iseven"] = par == 0
        rt["isodd"] = par == 1
        for rec in prog.ops[engname]:
            for s, v in rec["waits"].items():
                eng.wait_ge(semmap[s], v)
            if rec["fn"] is not None:
                r = rec["fn"](eng, rt)
                if rec["sem"] is not None:
                    hdl = semmap[rec["sem"]]
                    if isinstance(r, list):
                        each = rec["amt"] // len(r)
                        for ins in r:
                            ins.then_inc(hdl, each)
                    else:
                        r.then_inc(hdl, rec["amt"])

    with nc.Block() as block:
        @block.sync
        def _(sync):
            replay("sync", sync)

        @block.tensor
        def _(tensor):
            replay("tensor", tensor)

        @block.vector
        def _(vector):
            replay("vector", vector)

        @block.scalar
        def _(scalar):
            replay("scalar", scalar)

        @block.gpsimd
        def _(gpsimd):
            replay("gpsimd", gpsimd)

    est.close()
    return nc


_NC_CACHE = [None]


def _get_nc():
    if _NC_CACHE[0] is None:
        _NC_CACHE[0] = build_program()
    return _NC_CACHE[0]


def kernel(L0_attn_hidden_states, L0_attn_wq, L0_attn_wkv, L0_attn_wo,
           L0_ffn_A, L0_ffn_B, L0_ffn_Wr):
    x = np.asarray(L0_attn_hidden_states, dtype=np.float32)
    wq = np.asarray(L0_attn_wq, dtype=np.float32)
    wkv = np.asarray(L0_attn_wkv, dtype=np.float32)
    wo = np.ascontiguousarray(np.asarray(L0_attn_wo, dtype=np.float32))
    A = np.asarray(L0_ffn_A, dtype=np.float32)
    Bm = np.asarray(L0_ffn_B, dtype=np.float32)
    Wr = np.asarray(L0_ffn_Wr, dtype=np.float32)

    wq_used = wq.reshape(D, 32, 192)[:, :, :128]
    wk_full = wkv[:, :4096].reshape(D, 32, 128)
    wv_full = wkv[:, 4096:8192].reshape(D, 32, 128)
    aal = np.ascontiguousarray(
        A.transpose(2, 0, 1).reshape(D, KR)
        .reshape(2, 16, 128, 8, 128).transpose(3, 0, 2, 1, 4)
        .reshape(8, 2, 128, 2048))
    bal = np.ascontiguousarray(
        Bm.transpose(0, 2, 1).reshape(KR, D)
        .reshape(8, 128, 16, 256).transpose(2, 1, 0, 3).reshape(16, 128, 2048))
    wrt = np.ascontiguousarray(
        Wr.T.reshape(32, 128, 64).transpose(1, 0, 2).reshape(128, 2048))
    wo = np.ascontiguousarray(
        wo.reshape(2, 16, 128, 32, 128).transpose(3, 0, 2, 1, 4)
        .reshape(32, 2, 128, 2048))

    idn32 = np.eye(128, dtype=np.float32)
    onesrow = np.ones((1, 128), dtype=np.float32)
    onesrb = np.ones((1, 128), dtype=ml_dtypes.bfloat16)
    onesb = np.ones((128, 2), dtype=ml_dtypes.bfloat16)
    kk = np.arange(128)[:, None]
    uu = np.arange(896)[None, :]
    maskb = (uu >= kk + 384).astype(ml_dtypes.bfloat16)
    rep = (np.arange(KR)[None, :] // RANK ==
           np.arange(64)[:, None]).astype(ml_dtypes.bfloat16)
    nonce = np.random.randint(1, 2 ** 31 - 1, size=(1, 8)).astype(np.int32)

    in_maps = []
    for c in range(8):
        b, half = divmod(c, 2)
        hs = slice(NH * half, NH * (half + 1))
        xt = np.ascontiguousarray(x[b, T * half:T * (half + 1), :].T)
        wqkv = np.ascontiguousarray(np.stack([
            w[:, hs, :].reshape(D, NH * 128)
            .reshape(2, 16, 128, 16, 128).transpose(3, 0, 2, 1, 4)
            .reshape(NH, 2, 128, 2048)
            for w in (wq_used, wk_full, wv_full)]))
        in_maps.append(dict(
            xt=xt, wrt=wrt, aal=aal, bal=bal, wqkv=wqkv, wo=wo,
            idn32=idn32, onesrow=onesrow, onesrb=onesrb, onesb=onesb,
            maskb=maskb, rep=rep, nonce=nonce))

    nc = _get_nc()
    res = run_bass_kernel_spmd(nc, in_maps, core_ids=list(range(8)))
    out = np.empty((B, S, D), dtype=np.float32)
    for c in range(8):
        b, half = divmod(c, 2)
        out[b, T * half:T * (half + 1), :] = res.results[c]["out"].T
    return out
